# version 29
# baseline (speedup 1.0000x reference)
import contextlib
import os
import numpy as np

_SIM_BUILD = os.environ.get("BASS_SIM_FP64") == "1"
_A_CUT = int(os.environ.get("A_CUT", "99"))
_NSPAN_LIM = int(os.environ.get("NSPAN_LIM", "99"))
_G_CUT = int(os.environ.get("G_CUT", "99"))

import concourse.bass as bass
import concourse.mybir as mybir
from concourse.bass_utils import run_bass_kernel_spmd

F32 = mybir.dt.float32
BF16 = mybir.dt.bfloat16
I8 = mybir.dt.int8
AF = mybir.ActivationFunctionType
ALU = mybir.AluOpType
AX = mybir.AxisListType

T = 2048
D = 1024
DK = 512
DV = 1024
H = 4
HDK = 128
HDV = 256
M = 4
CAP = 1024
FF = 4096
SPAN = 512
NSPAN = T // SPAN
C = 128
CPS = SPAN // C
DC = D // 128
NPATH = 5
GN = 16.0
EPS = 1e-5
QSCALE = float(HDK) ** -0.5
NTILE = T // 128


class Sem:
    def __init__(self, h):
        self.h = h
        self.v = 0


class Prog:
    ENGINES = ("tensor", "vector", "scalar", "gpsimd", "sync")

    def __init__(self):
        self.streams = {e: [] for e in self.ENGINES}
        self.last_wait = {e: {} for e in self.ENGINES}
        self.maxw = {}
        self.auto_sems = {}
        self.prev_tok = {}
        self.sim_barrier = []

    def emit(self, eng, fn, waits=(), incs=(), is_dma=False, chain=True):
        if _SIM_BUILD and self.sim_barrier:
            waits = list(waits) + [(s, s.v) for s in self.sim_barrier]
        auto = self.auto_sems.get(eng)
        incs = list(incs)
        if auto is not None:
            if chain:
                prev = self.prev_tok.get(eng)
                if prev is not None:
                    waits = list(waits) + [prev]
            if not is_dma and not incs and chain:
                incs.append((auto, 1))
        ws = []
        for w in waits:
            if w is None:
                continue
            sem, val = w
            if val is None or val <= 0:
                continue
            if val > self.maxw.get(sem, 0):
                self.maxw[sem] = val
            if val > self.last_wait[eng].get(sem, 0):
                ws.append((sem, val))
                self.last_wait[eng][sem] = val
        toks = []
        inc_list = []
        for sem, n in incs:
            sem.v += n
            inc_list.append((sem, n))
            toks.append((sem, sem.v))
        def closure(eo, ws=ws, fn=fn, inc_list=inc_list):
            for sem, val in ws:
                eo.wait_ge(sem.h, val)
            op = fn(eo)
            for sem, n in inc_list:
                if op is not None:
                    op = op.then_inc(sem.h, n)
                else:
                    eo.sem_inc(sem.h, n)
        self.streams[eng].append(closure)
        if auto is not None and not is_dma and toks:
            self.prev_tok[eng] = toks[0]
        ret = [t for t in toks if t[0] is not auto] or toks
        return ret[0] if len(ret) == 1 else ret


class Pool:

    def __init__(self, t32, n32, t16, n16, t8, n8):
        self.t = {F32: t32, BF16: t16, I8: t8}
        self.n = {F32: n32, BF16: n16, I8: n8}
        self.off = {F32: 0, BF16: 0, I8: 0}

    def take(self, nelem, dtype, parts=128):
        isz = {F32: 4, BF16: 2, I8: 1}[dtype]
        al = max(1, 64 // isz)
        self.off[dtype] = (self.off[dtype] + al - 1) // al * al
        start = self.off[dtype]
        self.off[dtype] += nelem
        assert self.off[dtype] <= self.n[dtype], (
            f"pool overflow {dtype}: {self.off[dtype]} > {self.n[dtype]}")
        v = self.t[dtype][:, start:start + nelem]
        if parts != 128:
            v = v[0:parts]
        return v

    def f32(self, cols, parts=128):
        return self.take(cols, F32, parts)

    def bf16(self, cols, parts=128):
        return self.take(cols, BF16, parts)

    def i8(self, cols, parts=128):
        return self.take(cols, I8, parts)

    def fork(self):
        p = Pool(self.t[F32], self.n[F32], self.t[BF16], self.n[BF16],
                 self.t[I8], self.n[I8])
        p.off = dict(self.off)
        return p


def build_nc(debug=False, phases=5):
    nc = bass.Bass("TRN2", target_bir_lowering=False, detect_race_conditions=False)

    x_h = nc.dram_tensor("x", [T, D], F32, kind="ExternalInput").ap()
    wgate_h = nc.dram_tensor("w_gate", [D, M], F32, kind="ExternalInput").ap()
    wq_h = nc.dram_tensor("w_q", [D, DK], F32, kind="ExternalInput").ap()
    wk_h = nc.dram_tensor("w_k", [M, D, DK], F32, kind="ExternalInput").ap()
    wv_h = nc.dram_tensor("w_v", [M, D, DV], F32, kind="ExternalInput").ap()
    wgk1_h = nc.dram_tensor("w_gk1", [M, D, 16], F32, kind="ExternalInput").ap()
    wgk2_h = nc.dram_tensor("w_gk2", [M, 16, DK], F32, kind="ExternalInput").ap()
    bgk_h = nc.dram_tensor("b_gk", [M, DK], F32, kind="ExternalInput").ap()
    wks_h = nc.dram_tensor("w_ks", [D, DK], F32, kind="ExternalInput").ap()
    wvs_h = nc.dram_tensor("w_vs", [D, DV], F32, kind="ExternalInput").ap()
    wgk1s_h = nc.dram_tensor("w_gk1s", [D, 16], F32, kind="ExternalInput").ap()
    wgk2s_h = nc.dram_tensor("w_gk2s", [16, DK], F32, kind="ExternalInput").ap()
    bgks_h = nc.dram_tensor("b_gks", [DK], F32, kind="ExternalInput").ap()
    wg_h = nc.dram_tensor("w_g", [D, DV], F32, kind="ExternalInput").ap()
    wo_h = nc.dram_tensor("w_o", [DV, D], F32, kind="ExternalInput").ap()
    normw_h = nc.dram_tensor("norm_w", [HDV], F32, kind="ExternalInput").ap()
    w1_h = nc.dram_tensor("ffn_w1", [FF, D], F32, kind="ExternalInput").ap()
    b1_h = nc.dram_tensor("ffn_b1", [FF], F32, kind="ExternalInput").ap()
    w2_h = nc.dram_tensor("ffn_w2", [D, FF], F32, kind="ExternalInput").ap()
    b2_h = nc.dram_tensor("ffn_b2", [D], F32, kind="ExternalInput").ap()
    out_h = nc.dram_tensor("out", [T, D], F32, kind="ExternalOutput").ap()

    scr_kind = "ExternalOutput" if debug else "Internal"
    attn_hbm = nc.dram_tensor("attn_scr", [T, DV], BF16, kind=scr_kind).ap()
    x1_hbm = nc.dram_tensor("x1_scr", [T, D], F32, kind=scr_kind).ap()
    x1q_hbm = nc.dram_tensor("x1q_scr", [D, T], BF16, kind=scr_kind).ap()
    m2_hbm = nc.dram_tensor("m2_scr", [T, D], F32, kind=scr_kind).ap()
    hq_hbm = nc.dram_tensor("hq_scr", [T, FF], BF16, kind="Internal").ap()
    dbg = {}
    if debug:
        dbg["route"] = nc.dram_tensor("dbg_route", [4, M, T], F32,
                                      kind="ExternalOutput").ap()
        dbg["fstash"] = nc.dram_tensor("dbg_fstash", [128, 2 * NTILE], F32,
                                       kind="ExternalOutput").ap()
        dbg["scalew"] = nc.dram_tensor("dbg_scalew", [128, 4], F32,
                                       kind="ExternalOutput").ap()
        dbg["w1T"] = nc.dram_tensor("dbg_w1T", [128, DC * FF], BF16,
                                    kind="ExternalOutput").ap()
        dbg["h0"] = nc.dram_tensor("dbg_h0", [128, FF], BF16,
                                   kind="ExternalOutput").ap()

    es = contextlib.ExitStack()
    with es:
        N32 = 12992
        N16 = 77664
        N8 = 5120
        p32 = es.enter_context(nc.sbuf_tensor("p32", [128, N32], F32))
        p16 = es.enter_context(nc.sbuf_tensor("p16", [128, N16], BF16))
        p8 = es.enter_context(nc.sbuf_tensor("p8", [128, N8], I8))
        G = Pool(p32, N32, p16, N16, p8, N8)

        ident = G.bf16(128)
        identf = G.f32(128)
        trimask = G.bf16(128)
        ones_f = G.f32(128, parts=1)
        ones_b = G.bf16(128, parts=1)
        ones_col = G.f32(1)
        normw_row = G.f32(HDV, parts=1)
        normw_bc = G.bf16(HDV)
        wgate_l = G.f32(DC * M)
        bgk_sb = G.f32(NPATH * H)
        gkw1 = G.bf16(NPATH * DC * 16)
        gkw2 = G.bf16(NPATH * DK, parts=16)
        stat = G.f32(32)
        weff = G.bf16(T, parts=M)
        kept = G.bf16(T, parts=M)
        b1row = G.bf16(FF, parts=1)
        b2row = G.f32(D, parts=1)
        b2bc = G.f32(D)
        scalew = G.f32(4)
        x_tm = G.f32(2 * D)
        x_tmb = G.bf16(2 * D)
        selm = G.bf16(M * 128, parts=M)
        fstash = G.f32(2 * NTILE)

        A = G.fork()
        sg5 = A.bf16(NPATH * H * SPAN)
        qspan = A.bf16(H * SPAN)
        xfm = A.bf16(DC * SPAN)
        vtil = A.bf16(CPS * DV)
        wBs = A.bf16(M * SPAN)
        rBs = A.bf16(M * SPAN)
        scr_f = A.f32(H * SPAN)
        scr_b1 = A.bf16(H * SPAN)
        scr_b2 = A.bf16(H * SPAN)
        qw = A.bf16(H * SPAN)
        ktil = A.bf16(H * SPAN)
        khat = A.bf16(H * C)
        khat_tm = A.bf16(H * C)
        ebc5 = A.f32(NPATH * H * CPS)
        gklow = A.bf16(CPS * 16)
        gklow_slab = A.bf16(SPAN)
        gklow_fm = gklow_slab[0:16]
        wkbuf = A.bf16(DC * DK)
        wvbuf = A.bf16(DC * DV)
        o_acc = A.f32(CPS * DV)
        silu_sp = A.bf16(CPS * DV)
        S0 = A.bf16(NPATH * H * HDV)
        S1 = A.bf16(NPATH * H * HDV)
        ep_f1 = A.f32(DV)
        ep_b1 = A.bf16(DV)
        ep_b2x = A.bf16(DV)
        afm = ep_b2x
        ep_i8 = A.i8(D)

        R_ = G.fork()
        lg_fm = R_.f32(T, parts=M)
        pos_fm = R_.f32(T, parts=M)
        top2_fm = R_.bf16(T, parts=M)
        wpre_fm = R_.bf16(T, parts=M)
        rt_tm = R_.f32(16)
        xfm32 = R_.f32(DC * SPAN)


        Fp = G.fork()
        w1T = Fp.bf16(DC * FF)
        fwbuf = Fp.f32(4 * D)
        fqi8 = Fp.i8(D)
        fqbf = Fp.bf16(D)
        h_bf = Fp.bf16(FF)
        hsq = Fp.bf16(FF)
        hq_i8 = Fp.i8(FF)
        x1q_t = Fp.bf16(2 * D)
        wred = Fp.f32(34)
        simz = Fp.f32(D) if _SIM_BUILD else None
        simr = fwbuf[:, D:2 * D] if _SIM_BUILD else None
        F2 = G.fork()
        w2T = F2.bf16((FF // 128) * D)
        f2wbuf = F2.f32(2 * D)
        f2qi8 = F2.i8(D)
        f2qbf = F2.bf16(D)
        hq2_bf = F2.bf16(2 * FF)
        hq_fm = F2.bf16(2 * FF)
        m2_t = F2.f32(2 * D)
        fin_x1b = F2.f32(2 * D)
        fin_ob = F2.f32(D)
        simz2 = F2.f32(D) if _SIM_BUILD else None
        simr2 = f2wbuf[:, D:2 * D] if _SIM_BUILD else None

        L_ = G.fork()
        fin_m2 = L_.f32(2 * D)
        fin_x1 = L_.f32(2 * D)
        fin_s2 = L_.f32(D)
        fin_o = L_.f32(D)

        pbank = [es.enter_context(nc.psum_tensor(f"pb{i}", [128, 512], F32))
                 for i in range(7)]
        pbank.append(es.enter_context(nc.psum_tensor("pb7", [128, 1024], BF16)))

        def pbf(i):
            assert i < 7
            return pbank[i][:]

        def pbb(i):
            assert i == 7
            return pbank[7][:]

        tn = Sem(es.enter_context(nc.semaphore("tn")))
        vc = Sem(es.enter_context(nc.semaphore("vc")))
        sc = Sem(es.enter_context(nc.semaphore("sc")))
        gp = Sem(es.enter_context(nc.semaphore("gp")))
        so = Sem(es.enter_context(nc.semaphore("so")))
        _dma_names = ["sq", "gq", "s_x", "s_xr", "s_attnw", "s_attnr", "s_x1w",
                      "s_x1r", "s_x1qw", "s_x1qr", "s_m2w", "s_m2r", "s_fw",
                      "g_wk", "g_wv", "g_wq", "g_wg", "g_wo"]
        dsem = {n: Sem(es.enter_context(nc.semaphore(n))) for n in _dma_names}
        sq = dsem["sq"]
        gq = dsem["gq"]

        P = Prog()
        P.auto_sems = {"vector": vc, "scalar": sc, "gpsimd": gp}
        P.sim_barrier = [tn, vc, sc, gp]
        E = P.emit
        bank_tok = {}

        def use_banks(*ids):
            out = []
            for i in ids:
                out.extend(bank_tok.get(i, []))
            return out

        def done_banks(tok, *ids):
            for i in ids:
                bank_tok[i] = [tok]

        def also_banks(tok, *ids):
            for i in ids:
                bank_tok.setdefault(i, []).append(tok)

        def DMA(queue, out, in_, sem, waits=()):
            ws = list(waits) + [(sem, sem.v if _SIM_BUILD else P.maxw.get(sem, 0))]
            if _SIM_BUILD:
                ws += [(tn, tn.v), (vc, vc.v), (sc, sc.v), (gp, gp.v)]
            return E(queue, lambda eo, o=out, i=in_: eo.dma_start(out=o, in_=i),
                     waits=ws, incs=[(sem, 16)], is_dma=True)

        def MM(out, lhsT, rhs, start, stop, waits=(), inc=False):
            return E("tensor",
                     lambda eo, o=out, l=lhsT, r=rhs, a=start, b=stop:
                     eo.matmul(o, l, r, start=a, stop=b),
                     waits=waits, incs=([(tn, 1)] if inc else ()))

        def TP(out, in_, idm, waits=(), inc=False):
            return E("tensor",
                     lambda eo, o=out, i=in_, d=idm: eo.transpose(o, i, d),
                     waits=waits, incs=([(tn, 1)] if inc else ()))

        def TNOP(waits=()):
            return E("tensor", lambda eo: None, waits=waits, incs=[(tn, 1)])

        def VCPY(out, in_, waits=(), inc=False, chain=True):
            return E("vector", lambda eo, o=out, i=in_: eo.tensor_copy(out=o, in_=i),
                     waits=waits, incs=([(vc, 1)] if inc else ()), chain=chain)

        def VTT(out, in0, in1, op, waits=(), inc=False, chain=True):
            return E("vector",
                     lambda eo, o=out, a=in0, b=in1, p=op:
                     eo.tensor_tensor(out=o, in0=a, in1=b, op=p),
                     waits=waits, incs=([(vc, 1)] if inc else ()), chain=chain)

        def VTS(out, in0, s1, s2, op0, op1=None, waits=(), inc=False):
            if op1 is None:
                return E("vector",
                         lambda eo, o=out, a=in0, x=s1, p=op0:
                         eo.tensor_scalar(out=o, in0=a, scalar1=x, scalar2=None, op0=p),
                         waits=waits, incs=([(vc, 1)] if inc else ()))
            return E("vector",
                     lambda eo, o=out, a=in0, x=s1, y=s2, p=op0, q=op1:
                     eo.tensor_scalar(out=o, in0=a, scalar1=x, scalar2=y, op0=p, op1=q),
                     waits=waits, incs=([(vc, 1)] if inc else ()))

        def VSTT(out, in0, scalar, in1, op0, op1, waits=(), inc=False, chain=True):
            return E("vector",
                     lambda eo, o=out, a=in0, s=scalar, b=in1, p=op0, q=op1:
                     eo.scalar_tensor_tensor(out=o, in0=a, scalar=s, in1=b, op0=p, op1=q),
                     waits=waits, incs=([(vc, 1)] if inc else ()), chain=chain)

        def VRED(out, in_, op, absval=False, waits=(), inc=False):
            return E("vector",
                     lambda eo, o=out, i=in_, p=op, a=absval:
                     eo.tensor_reduce(out=o, in_=i, axis=AX.X, op=p,
                                      apply_absolute_value=a),
                     waits=waits, incs=([(vc, 1)] if inc else ()))

        def VSCAN(out, d0, d1, waits=(), inc=False, chain=True):
            return E("vector",
                     lambda eo, o=out, a=d0, b=d1:
                     eo.tensor_tensor_scan(out=o, data0=a, data1=b, initial=0.0,
                                           op0=ALU.add, op1=ALU.bypass),
                     waits=waits, incs=([(vc, 1)] if inc else ()), chain=chain)

        def VRCP(out, in_, waits=(), inc=False):
            return E("vector", lambda eo, o=out, i=in_: eo.reciprocal(out=o, in_=i),
                     waits=waits, incs=([(vc, 1)] if inc else ()))

        def VNOP(waits=()):
            return E("vector", lambda eo: None, waits=waits, incs=[(vc, 1)])

        def ACT(out, in_, func, bias=None, scale=None, accum=None, waits=(), inc=False,
                chain=True):
            def f(eo, o=out, i=in_, fu=func, b=bias, s=scale, a=accum):
                kw = {}
                if b is not None:
                    kw["bias"] = b
                if s is not None:
                    kw["scale"] = s
                if a is not None:
                    kw["accum_out"] = a
                return eo.activation(out=o, in_=i, func=fu, **kw)
            return E("scalar", f, waits=waits, incs=([(sc, 1)] if inc else ()), chain=chain)

        def SNOP(waits=()):
            return E("scalar", lambda eo: None, waits=waits, incs=[(sc, 1)])

        def join():
            toks = [(tn, tn.v), (vc, vc.v), (sc, sc.v), (gp, gp.v)]
            toks += [(s, s.v) for s in dsem.values()]
            for eng in Prog.ENGINES:
                E(eng, lambda eo: None, waits=toks)
            bank_tok.clear()

        t1 = E("gpsimd", lambda eo: eo.memset(ident[:], 0.0), incs=[(gp, 1)])
        t2 = E("gpsimd", lambda eo: eo.memset(identf[:], 0.0), incs=[(gp, 1)])
        t3 = E("gpsimd", lambda eo: eo.memset(trimask[:], 1.0), incs=[(gp, 1)])
        t4 = E("gpsimd", lambda eo: eo.memset(selm[:], 1.0), incs=[(gp, 1)])
        E("gpsimd", lambda eo: eo.memset(ones_f[:], 1.0))
        E("gpsimd", lambda eo: eo.memset(ones_b[:], 1.0))
        E("gpsimd", lambda eo: eo.memset(ones_col[:], 1.0), incs=[(gp, 1)])
        E("gpsimd", lambda eo: eo.affine_select(out=ident[:], in_=ident[:],
                                                pattern=[[-1, 128]], compare_op=ALU.not_equal,
                                                fill=1.0, base=0, channel_multiplier=1),
          waits=[t1], incs=[(gp, 1)])
        E("gpsimd", lambda eo: eo.affine_select(out=identf[:], in_=identf[:],
                                                pattern=[[-1, 128]], compare_op=ALU.not_equal,
                                                fill=1.0, base=0, channel_multiplier=1),
          waits=[t2], incs=[(gp, 1)])
        E("gpsimd", lambda eo: eo.affine_select(out=trimask[:], in_=trimask[:],
                                                pattern=[[1, 128]], compare_op=ALU.is_ge,
                                                fill=0.0, base=0, channel_multiplier=-1),
          waits=[t3], incs=[(gp, 1)])
        t5 = E("gpsimd", lambda eo: eo.affine_select(out=selm[:], in_=selm[:],
                                                pattern=[[1, M * 128]], compare_op=ALU.is_ge,
                                                fill=0.0, base=0, channel_multiplier=-128),
               waits=[t4], incs=[(gp, 1)])
        tok_cst = E("gpsimd", lambda eo: eo.affine_select(out=selm[:], in_=selm[:],
                                                          pattern=[[-1, M * 128]],
                                                          compare_op=ALU.is_ge,
                                                          fill=0.0, base=127,
                                                          channel_multiplier=128),
                    waits=[t5], incs=[(gp, 1)])

        DMA("sync", wgate_l[:].rearrange("p (c m) -> p c m", c=DC),
            wgate_h.rearrange("(c p) m -> p c m", p=128), sq)
        DMA("sync", normw_row[:], normw_h[None, :], sq)
        DMA("sync", b2row[:], b2_h[None, :], sq)
        DMA("gpsimd", b1row[:], b1_h[None, :], gq)
        for m in range(M):
            DMA("gpsimd", gkw1[:, m * DC * 16:(m + 1) * DC * 16]
                .rearrange("p (c l) -> p c l", c=DC),
                wgk1_h.rearrange("m (c p) l -> m p c l", p=128)[m], gq)
        DMA("gpsimd", gkw1[:, DC * 16 * M:].rearrange("p (c l) -> p c l", c=DC),
            wgk1s_h.rearrange("(c p) l -> p c l", p=128), gq)
        for m in range(M):
            DMA("gpsimd", gkw2[:, m * DK:(m + 1) * DK], wgk2_h[m], gq)
        tok_gkw = DMA("gpsimd", gkw2[:, M * DK:], wgk2s_h[:, :], gq)
        tok_sq0 = (sq, sq.v)

        t_nw = MM(pbf(0)[:, 0:HDV], ones_f[:], normw_row[:], True, True,
                  waits=[tok_sq0, tok_cst] + use_banks(0), inc=True)
        t_b2a = MM(pbf(1)[:, 0:512], ones_f[:], b2row[:, 0:512], True, True,
                   waits=use_banks(1), inc=True)
        t_b2b = MM(pbf(2)[:, 0:512], ones_f[:], b2row[:, 512:1024], True, True,
                   waits=use_banks(2), inc=True)
        for p in range(M):
            for kt in range(H):
                DMA("sync", bgk_sb[:, (p * H + kt):(p * H + kt) + 1],
                    bgk_h[p, kt * 128:(kt + 1) * 128][:, None], sq)
        for kt in range(H):
            DMA("sync", bgk_sb[:, (M * H + kt):(M * H + kt) + 1],
                bgks_h[kt * 128:(kt + 1) * 128][:, None], sq)
        tok_bgk = (sq, sq.v)
        tok_nw = ACT(normw_bc[:], pbf(0)[:, 0:HDV], AF.Copy, waits=[t_nw], inc=True)
        VCPY(b2bc[:, 0:512], pbf(1)[:, 0:512], waits=[t_b2a])
        tok_b2 = VCPY(b2bc[:, 512:1024], pbf(2)[:, 0:512], waits=[t_b2b], inc=True)
        done_banks(tok_nw, 0)
        done_banks(tok_b2, 1, 2)

        xprev = {}
        xprev2 = {}

        def build_xfm(s):
            last = None
            for tt in range(CPS):
                par = tt % 2
                xbdst = x_tmb[:, par * D:(par + 1) * D]
                tok_c = DMA("gpsimd", xbdst,
                            x_h[(s * CPS + tt) * 128:(s * CPS + tt + 1) * 128, :],
                            dsem["s_attnr"], waits=[xprev2.get(par)])
                xprev[par] = tok_c
                for g4 in range(2):
                    tk = None
                    for dc4 in range(4):
                        dc = g4 * 4 + dc4
                        tk = TP(pbb(7)[:, dc4 * 128:(dc4 + 1) * 128],
                                xbdst[:, dc * 128:(dc + 1) * 128],
                                ident[:],
                                waits=[tok_c] + (use_banks(7) if dc4 == 0 else []),
                                inc=(dc4 == 3))
                    dst = xfm[:].rearrange("p (c t) -> p c t", c=DC)[
                        :, g4 * 4:(g4 + 1) * 4, tt * 128:(tt + 1) * 128]
                    last = ACT(dst, pbb(7)[:, 0:512].rearrange("p (a b) -> p a b", a=4),
                               AF.Copy, waits=[tk], inc=True, chain=False)
                    done_banks(last, 7)
                    if g4 == 1:
                        xprev2[par] = tk
            return last

        xprev3 = {}
        xprev4 = {}
        for s in range(NSPAN):
            last = None
            for tt in range(CPS):
                par = tt % 2
                xdst = x_tm[:, par * D:(par + 1) * D]
                tok_x = DMA("sync", xdst,
                            x_h[(s * CPS + tt) * 128:(s * CPS + tt + 1) * 128, :],
                            dsem["s_x"], waits=[xprev3.get(par)])
                for g4 in range(2):
                    tk = None
                    for dc4 in range(4):
                        dc = g4 * 4 + dc4
                        tk = TP(pbf(6)[:, dc4 * 128:(dc4 + 1) * 128],
                                xdst[:, dc * 128:(dc + 1) * 128], identf[:],
                                waits=[tok_x] + (use_banks(6) if dc4 == 0 else []),
                                inc=(dc4 == 3))
                    dst = xfm32[:].rearrange("p (c t) -> p c t", c=DC)[
                        :, g4 * 4:(g4 + 1) * 4, tt * 128:(tt + 1) * 128]
                    last = VCPY(dst, pbf(6)[:, 0:512].rearrange("p (a b) -> p a b", a=4),
                                waits=[tk], inc=True)
                    done_banks(last, 6)
                    if g4 == 1:
                        xprev3[par] = tk
            tok_xfm = last
            tk = None
            for dc in range(DC):
                tk = MM(pbf(0)[0:M, 0:SPAN], wgate_l[:, dc * M:(dc + 1) * M],
                        xfm32[:, dc * SPAN:(dc + 1) * SPAN], dc == 0, dc == DC - 1,
                        waits=[tok_xfm, tok_sq0] + (use_banks(0) if dc == 0 else []),
                        inc=(dc == DC - 1))
            tok_lg = VCPY(lg_fm[:, s * SPAN:(s + 1) * SPAN], pbf(0)[0:M, 0:SPAN],
                          waits=[tk], inc=True)
            done_banks(tok_lg, 0)

        for t16 in range(NTILE):
            tk = TP(pbf(0)[:, 0:M], lg_fm[:, t16 * 128:(t16 + 1) * 128], identf[0:M, 0:M],
                    waits=[(vc, vc.v)] + use_banks(0), inc=True)
            tok_c = VCPY(rt_tm[:, 0:4], pbf(0)[:, 0:M], waits=[tk], inc=True)
            done_banks(tok_c, 0)
            VRED(stat[:, 0:1], rt_tm[:, 0:4], ALU.max)
            VTS(stat[:, 4:8], rt_tm[:, 0:4], stat[:, 0:1], None, ALU.is_equal)
            VSTT(stat[:, 4:8], stat[:, 4:8], -1e30, rt_tm[:, 0:4], ALU.mult, ALU.add)
            VRED(stat[:, 1:2], stat[:, 4:8], ALU.max)
            VTS(stat[:, 4:8], rt_tm[:, 0:4], stat[:, 1:2], None, ALU.is_ge)
            tok_m = VTS(stat[:, 2:3], stat[:, 0:1], -1.0, None, ALU.mult, inc=True)
            tok_e = ACT(stat[:, 8:12], rt_tm[:, 0:4], AF.Exp, bias=stat[:, 2:3],
                        waits=[tok_m], inc=True)
            VTT(stat[:, 12:16], stat[:, 8:12], stat[:, 4:8], ALU.mult, waits=[tok_e])
            VRED(stat[:, 3:4], stat[:, 12:16], ALU.add)
            VRCP(stat[:, 2:3], stat[:, 3:4])
            VCPY(rt_tm[:, 4:8], stat[:, 4:8])
            tok_w = VTS(rt_tm[:, 8:12], stat[:, 12:16], stat[:, 2:3], None, ALU.mult,
                        inc=True)
            TP(pbf(0)[0:4, 0:128], rt_tm[:, 4:8], identf[:], waits=[tok_w] + use_banks(0))
            tk2 = TP(pbf(0)[0:4, 128:256], rt_tm[:, 8:12], identf[:], inc=True)
            VCPY(top2_fm[:, t16 * 128:(t16 + 1) * 128], pbf(0)[0:M, 0:128], waits=[tk2])
            tok_f = VCPY(wpre_fm[:, t16 * 128:(t16 + 1) * 128], pbf(0)[0:M, 128:256],
                         inc=True)
            done_banks(tok_f, 0)

        VSCAN(pos_fm[:], top2_fm[:], top2_fm[:])
        VTS(stat[0:M, 16:17], pos_fm[:, T - 1:T], float(-CAP), None, ALU.add)
        VSTT(kept[:], pos_fm[:], stat[0:M, 16:17], top2_fm[:], ALU.is_gt, ALU.mult)
        tok_route = VTT(weff[:], wpre_fm[:], kept[:], ALU.mult, inc=True)
        if debug:
            DMA("sync", dbg["route"][0], lg_fm[:], dsem["s_attnw"], waits=[tok_route])
            DMA("sync", dbg["route"][1], pos_fm[:], dsem["s_attnw"], waits=[tok_route])
            DMA("gpsimd", dbg["route"][2], kept[:], gq, waits=[tok_route])
            DMA("gpsimd", dbg["route"][3], weff[:], gq, waits=[tok_route])
        join()

        if phases < 2:
            _finish(nc, P, so)
            return nc
        E("vector", lambda eo: eo.memset(S0[:], 0.0), incs=[(vc, 1)])
        state_tok = [{0: (vc, vc.v), 1: (vc, vc.v)} for _ in range(NPATH)]
        oacc_free = {}
        xr_war = {}
        afm_war = {}
        epb1_war = {}
        km_war = {}
        store_war = {}
        sil_war = {}
        gate_tok = {}

        nspan_eff = min(NSPAN, _NSPAN_LIM)

        def prep_span(sp, tok_xfm_sp):
            for m in range(M):
                MM(pbf(m)[:, 0:SPAN], selm[:, m * 128:(m + 1) * 128],
                   weff[:, sp * SPAN:(sp + 1) * SPAN], True, True,
                   waits=[tok_route, tok_cst] + use_banks(m),
                   inc=(m == M - 1))
            tkw = (tn, tn.v)
            for m in range(M):
                t1 = ACT(wBs[:, m * SPAN:(m + 1) * SPAN], pbf(m)[:, 0:SPAN], AF.Copy,
                         waits=[tkw] + ([(vc, vc.v)] if m == 0 else []),
                         inc=True, chain=(m == 0))
                done_banks(t1, m)
            for m in range(M):
                MM(pbf(m)[:, 0:SPAN], selm[:, m * 128:(m + 1) * 128],
                   kept[:, sp * SPAN:(sp + 1) * SPAN], True, True,
                   waits=use_banks(m), inc=(m == M - 1))
            tkr = (tn, tn.v)
            for m in range(M):
                t2 = ACT(rBs[:, m * SPAN:(m + 1) * SPAN], pbf(m)[:, 0:SPAN], AF.Copy,
                         waits=[tkr], inc=True, chain=(m == 0))
                done_banks(t2, m)
            tok_bc_ = (sc, sc.v)
            tok_wq = DMA("gpsimd", wkbuf[:].rearrange("p (c k) -> p c k", c=DC),
                         wq_h.rearrange("(c p) k -> p c k", p=128), dsem["g_wk"],
                         waits=[(tn, tn.v)])
            for kt in range(H):
                tk = None
                for dc in range(DC):
                    tk = MM(pbf(2)[:, 0:SPAN],
                            wkbuf[:, dc * DK + kt * 128: dc * DK + kt * 128 + 128],
                            xfm[:, dc * SPAN:(dc + 1) * SPAN], dc == 0, dc == DC - 1,
                            waits=[tok_wq, tok_xfm_sp] + (use_banks(2) if dc == 0 else []),
                            inc=(dc == DC - 1))
                tq = ACT(qspan[:, kt * SPAN:(kt + 1) * SPAN], pbf(2)[:, 0:SPAN], AF.Copy,
                         scale=QSCALE,
                         waits=[tk] + ([(vc, vc.v)] if kt == 0 else []), inc=True)
                done_banks(tq, 2)
            return tok_bc_, (sc, sc.v)

        tok_xfm_next = build_xfm(0)
        for s in range(nspan_eff):
            tok_xfm = tok_xfm_next
            if _A_CUT < 1:
                continue

            if s == 0:
                tok_bc, tok_q = prep_span(0, tok_xfm)
            else:
                tok_bc, tok_q = next_bc, next_q
            if _A_CUT < 3:
                continue

            tok_wg = DMA("gpsimd", wvbuf[:].rearrange("p (c k) -> p c k", c=DC),
                         wg_h.rearrange("(c p) k -> p c k", p=128), dsem["g_wv"],
                         waits=[(tn, tn.v)])
            for tt in range(CPS):
                for half in range(2):
                    bk = 3 + half
                    tk = None
                    for dc in range(DC):
                        tk = MM(pbf(bk)[:, 0:512],
                                xfm[:, dc * SPAN + tt * 128: dc * SPAN + tt * 128 + 128],
                                wvbuf[:, dc * DV + half * 512: dc * DV + half * 512 + 512],
                                dc == 0, dc == DC - 1,
                                waits=[tok_wg, tok_xfm] + (use_banks(bk) if dc == 0 else []),
                                inc=(dc == DC - 1))
                    tv_ = VCPY(ep_b1[:, half * 512:(half + 1) * 512], pbf(bk)[:, 0:512],
                               waits=[tk, epb1_war.get(0), store_war.get(0)], inc=True)
                    ts_ = ACT(ep_b2x[:, half * 512:(half + 1) * 512],
                              pbf(bk)[:, 0:512], AF.Sigmoid,
                              waits=[tk, tv_, afm_war.get(0)], inc=True)
                    done_banks(ts_, bk)
                if _G_CUT >= 3:
                    VTT(silu_sp[:, tt * DV:(tt + 1) * DV], ep_b1[:],
                        ep_b2x[:], ALU.mult,
                        waits=[(sc, sc.v), sil_war.get(0)], inc=True)
            tok_silu = (vc, vc.v)
            if _A_CUT < 4:
                continue

            for p in range(NPATH):
                tk = None
                for tt in range(CPS):
                    for dc in range(DC):
                        tk = MM(pbf(0)[:, tt * 16:(tt + 1) * 16],
                                xfm[:, dc * SPAN + tt * 128: dc * SPAN + tt * 128 + 128],
                                gkw1[:, (p * DC + dc) * 16:(p * DC + dc + 1) * 16],
                                tt == 0 and dc == 0, tt == CPS - 1 and dc == DC - 1,
                                waits=[tok_gkw, tok_xfm] + (use_banks(0) if (tt, dc) == (0, 0) else []),
                                inc=(tt == CPS - 1 and dc == DC - 1))
                tl = VCPY(gklow[:], pbf(0)[:, 0:CPS * 16], waits=[tk], inc=True)
                done_banks(tl, 0)
                tk = None
                for tt in range(CPS):
                    tk = TP(pbb(7)[0:16, tt * 128:(tt + 1) * 128],
                            gklow[:, tt * 16:(tt + 1) * 16], ident[:],
                            waits=[(vc, vc.v)] + (use_banks(7) if tt == 0 else []),
                            inc=(tt == CPS - 1))
                tok_low = VCPY(gklow_fm[:], pbb(7)[0:16, 0:SPAN], waits=[tk], inc=True)
                done_banks(tok_low, 7)
                for kt in range(H):
                    tk = MM(pbf(0)[:, 0:SPAN],
                            gkw2[:, p * DK + kt * 128: p * DK + kt * 128 + 128],
                            gklow_fm[:], True, True, waits=[tok_low] + use_banks(0),
                            inc=True)
                    tsg = ACT(sg5[:, (p * H + kt) * SPAN:(p * H + kt + 1) * SPAN],
                              pbf(0)[:, 0:SPAN], AF.Sigmoid,
                              bias=bgk_sb[:, p * H + kt: p * H + kt + 1],
                              waits=[tk, tok_bgk], inc=True)
                    done_banks(tsg, 0)
            tok_sg = (sc, sc.v)
            if _A_CUT < 5:
                continue

            def srck(p):
                return (wk_h.rearrange("m (c p) k -> m p c k", p=128)[p] if p < M
                        else wks_h.rearrange("(c p) k -> p c k", p=128))

            def srcv(p):
                return (wv_h.rearrange("m (c p) k -> m p c k", p=128)[p] if p < M
                        else wvs_h.rearrange("(c p) k -> p c k", p=128))

            khwar = {}
            tok_wk = DMA("gpsimd", wkbuf[:].rearrange("p (c k) -> p c k", c=DC),
                         srck(0), dsem["g_wk"], waits=[tok_q])
            tok_wv = DMA("gpsimd", wvbuf[:].rearrange("p (c k) -> p c k", c=DC),
                         srcv(0), dsem["g_wv"], waits=[tok_silu])
            for p in range(NPATH):
                tok_ln = ACT(scr_f[:], sg5[:, p * H * SPAN:(p + 1) * H * SPAN], AF.Ln,
                             waits=[tok_sg, (vc, vc.v)], inc=True)
                if p < M:
                    rBp = rBs[:, p * SPAN:(p + 1) * SPAN].unsqueeze(1).broadcast_to(
                        [128, H, SPAN])
                    tok_gh = VSTT(scr_b1[:], scr_f[:], 1.0 / GN, rBp, ALU.mult, ALU.mult,
                                  waits=[tok_ln, tok_bc], inc=True)
                else:
                    tok_gh = VTS(scr_b1[:], scr_f[:], 1.0 / GN, None, ALU.mult,
                                 waits=[tok_ln], inc=True)
                tok_b = None
                for kt in range(H):
                    for ch in range(CPS):
                        cs = slice(kt * SPAN + ch * C, kt * SPAN + (ch + 1) * C)
                        tok_b = VSCAN(scr_f[:, cs], scr_b1[:, cs], scr_b1[:, cs],
                                      waits=[tok_gh],
                                      inc=(kt == H - 1 and ch == CPS - 1),
                                      chain=(kt == 0 and ch == 0))
                ACT(ebc5[:, p * H * CPS:(p + 1) * H * CPS],
                    scr_f[:].rearrange("p (n c) -> p n c", c=C)[:, :, C - 1:C], AF.Exp,
                    waits=[tok_b], inc=True)
                tok_eb = ACT(scr_b1[:], scr_f[:], AF.Exp, waits=[tok_b], inc=True,
                             chain=False)
                tok_emb = ACT(scr_b2[:], scr_f[:], AF.Exp, scale=-1.0, inc=True,
                              chain=False)
                if p < M:
                    wBp = wBs[:, p * SPAN:(p + 1) * SPAN].unsqueeze(1).broadcast_to(
                        [128, H, SPAN])
                    VTT(scr_b1[:], scr_b1[:], wBp, ALU.mult, waits=[tok_eb])
                    VTT(qw[:], scr_b1[:], qspan[:], ALU.mult, waits=[tok_q], inc=True)
                    rBp = rBs[:, p * SPAN:(p + 1) * SPAN].unsqueeze(1).broadcast_to(
                        [128, H, SPAN])
                    tok_embM = VTT(scr_b2[:], scr_b2[:], rBp, ALU.mult,
                                   waits=[tok_emb], inc=True)
                else:
                    VTT(qw[:], scr_b1[:], qspan[:], ALU.mult,
                        waits=[tok_eb, tok_emb, tok_q], inc=True)
                    tok_embM = (vc, vc.v)

                if _A_CUT < 6:
                    continue
                tok_kproj = None
                for kt in range(H):
                    bk = 5 + kt % 2
                    tk = None
                    for dc in range(DC):
                        tk = MM(pbf(bk)[:, 0:SPAN],
                                wkbuf[:, dc * DK + kt * 128: dc * DK + kt * 128 + 128],
                                xfm[:, dc * SPAN:(dc + 1) * SPAN], dc == 0, dc == DC - 1,
                                waits=[tok_wk] + (use_banks(bk) if dc == 0 else []),
                                inc=(dc == DC - 1))
                    tok_kproj = tk
                    te = ACT(ktil[:, kt * SPAN:(kt + 1) * SPAN], pbf(bk)[:, 0:SPAN],
                             AF.Copy,
                             waits=[tk] + ([km_war.get(0)] if kt == 0 else []),
                             inc=True, chain=False)
                    done_banks(te, 5 + kt % 2)
                tok_kmask = VTT(ktil[:], ktil[:], scr_b2[:], ALU.mult,
                                waits=[tok_embM, te], inc=True)
                km_war[0] = tok_kmask
                tok_vproj = None
                for tt in range(CPS):
                    for half in range(2):
                        bk = 5 + half
                        tk = None
                        for dc in range(DC):
                            tk = MM(pbf(bk)[:, 0:512],
                                    xfm[:, dc * SPAN + tt * 128: dc * SPAN + tt * 128 + 128],
                                    wvbuf[:, dc * DV + half * 512: dc * DV + half * 512 + 512],
                                    dc == 0, dc == DC - 1,
                                    waits=[tok_wv] + (use_banks(bk) if dc == 0 else []),
                                    inc=(dc == DC - 1))
                        tok_vproj = tk
                        tv = ACT(vtil[:, tt * DV + half * 512: tt * DV + half * 512 + 512],
                                 pbf(bk)[:, 0:512], AF.Copy, waits=[tk], inc=True,
                                 chain=False)
                        done_banks(tv, bk)
                tok_v = (sc, sc.v)
                if p + 1 < NPATH:
                    tok_wk = DMA("gpsimd", wkbuf[:].rearrange("p (c k) -> p c k", c=DC),
                                 srck(p + 1), dsem["g_wk"], waits=[tok_kproj])
                    tok_wv = DMA("gpsimd", wvbuf[:].rearrange("p (c k) -> p c k", c=DC),
                                 srcv(p + 1), dsem["g_wv"], waits=[tok_vproj])

                if _A_CUT < 7:
                    continue
                ktm_all = scr_b1
                asb_all = scr_b2
                for ch in range(CPS):
                    kbuf = khat if ch % 2 == 0 else khat_tm
                    tok_kh = None
                    for hh in range(H):
                        k_ap = ktil[:, hh * SPAN + ch * C: hh * SPAN + ch * C + C]
                        ecol = ebc5[:, (p * H + hh) * CPS + ch:(p * H + hh) * CPS + ch + 1]
                        tok_kh = ACT(kbuf[:, hh * C:(hh + 1) * C], k_ap, AF.Copy,
                                     scale=ecol,
                                     waits=([tok_kmask, khwar.get(ch % 2)] if hh == 0 else []),
                                     inc=(hh == H - 1), chain=(hh == 0 and ch == 0))
                    tk_t = None
                    for hh in range(H):
                        tk_t = TP(pbb(7)[:, hh * C:(hh + 1) * C],
                                  kbuf[:, hh * C:(hh + 1) * C], ident[:],
                                  waits=[tok_kh] + (use_banks(7) if hh == 0 else []),
                                  inc=(hh == H - 1))
                    khwar[ch % 2] = tk_t
                    tktm = VCPY(ktm_all[:, ch * 512:(ch + 1) * 512], pbb(7)[:, 0:512],
                                waits=[tk_t], inc=True, chain=(ch == 0))
                    done_banks(tktm, 7)
                for ch in range(CPS):
                    tk_a = None
                    for hh in range(H):
                        q_ap = qw[:, hh * SPAN + ch * C: hh * SPAN + ch * C + C]
                        k_ap = ktil[:, hh * SPAN + ch * C: hh * SPAN + ch * C + C]
                        tk_a = MM(pbf(2)[:, hh * C:(hh + 1) * C], k_ap, q_ap,
                                  hh == 0, hh == H - 1,
                                  waits=([tok_kmask] if hh == 0 else [])
                                        + (use_banks(2) if hh == 0 else []),
                                  inc=(hh == H - 1))
                    trim4 = trimask[:, :].unsqueeze(1).broadcast_to([128, H, C])
                    tok_am = VTT(asb_all[:, ch * 512:(ch + 1) * 512], pbf(2)[:, 0:512],
                                 trim4, ALU.mult, waits=[tk_a], inc=True, chain=False)
                    done_banks(tok_am, 2)
                tok_pre = (vc, vc.v)
                for ch in range(CPS):
                    gchunk = s * CPS + ch
                    Sprev = S0 if gchunk % 2 == 0 else S1
                    Snext = S1 if gchunk % 2 == 0 else S0
                    tk_s = None
                    for hh in range(H):
                        v_ap = vtil[:, ch * DV + hh * HDV: ch * DV + hh * HDV + HDV]
                        bk = 3 + hh // 2
                        osl = slice((hh % 2) * HDV, (hh % 2) * HDV + HDV)
                        tk_s = MM(pbf(bk)[:, osl],
                                  ktm_all[:, ch * 512 + hh * C: ch * 512 + (hh + 1) * C],
                                  v_ap, hh % 2 == 0, hh % 2 == 1,
                                  waits=[tok_pre] + (use_banks(3, 4) if hh == 0 else []),
                                  inc=(hh == H - 1))
                    for hh in range(H):
                        bk = 3 + hh // 2
                        osl = slice((hh % 2) * HDV, (hh % 2) * HDV + HDV)
                        ecol = ebc5[:, (p * H + hh) * CPS + ch:(p * H + hh) * CPS + ch + 1]
                        sp_ap = Sprev[:, (p * H + hh) * HDV:(p * H + hh + 1) * HDV]
                        sn_ap = Snext[:, (p * H + hh) * HDV:(p * H + hh + 1) * HDV]
                        tms = VSTT(sn_ap, sp_ap, ecol, pbf(bk)[:, osl], ALU.mult, ALU.add,
                                   waits=[tk_s], inc=(hh == H - 1), chain=(hh == 0))
                    done_banks(tms, 3, 4)
                    state_tok[p][(ch + 1) % 2] = tms
                    for hh in range(H):
                        q_ap = qw[:, hh * SPAN + ch * C: hh * SPAN + ch * C + C]
                        s_ap = Sprev[:, (p * H + hh) * HDV:(p * H + hh + 1) * HDV]
                        bk = hh // 2
                        osl = slice((hh % 2) * HDV, (hh % 2) * HDV + HDV)
                        MM(pbf(bk)[:, osl], q_ap, s_ap, hh % 2 == 0, False,
                           waits=[tok_v, state_tok[p][gchunk % 2]]
                                 + (use_banks(0, 1) if hh == 0 else []))
                    tk_o = None
                    for hh in range(H):
                        v_ap = vtil[:, ch * DV + hh * HDV: ch * DV + hh * HDV + HDV]
                        bk = hh // 2
                        osl = slice((hh % 2) * HDV, (hh % 2) * HDV + HDV)
                        tk_o = MM(pbf(bk)[:, osl],
                                  asb_all[:, ch * 512 + hh * C: ch * 512 + (hh + 1) * C],
                                  v_ap, False, hh % 2 == 1,
                                  waits=[tok_pre], inc=(hh == H - 1))
                    for half in range(2):
                        osl = slice(ch * DV + half * 512, ch * DV + half * 512 + 512)
                        if p == 0:
                            to = VCPY(o_acc[:, osl], pbf(half)[:, 0:512],
                                      waits=[tk_o] + (oacc_free.get(ch) or []),
                                      inc=True, chain=(half == 0))
                        else:
                            to = VTT(o_acc[:, osl], o_acc[:, osl], pbf(half)[:, 0:512],
                                     ALU.add, waits=[tk_o], inc=True, chain=(half == 0))
                        done_banks(to, half)

            if _A_CUT < 8:
                join()
                continue
            tok_wo = DMA("gpsimd", wvbuf[:].rearrange("p (c k) -> p c k", c=DC),
                         wo_h.rearrange("(c p) k -> p c k", p=128), dsem["g_wo"],
                         waits=[tok_vproj])
            tok_xfm_next = build_xfm(s + 1) if s + 1 < nspan_eff else None
            if s + 1 < nspan_eff:
                next_bc, next_q = prep_span(s + 1, tok_xfm_next)
            for ch in range(CPS):
                tok_oacc = (vc, vc.v)
                tok_st = None
                for hh in range(H):
                    osl = slice(ch * DV + hh * HDV, ch * DV + hh * HDV + HDV)
                    tok_st = ACT(ep_f1[:, hh * HDV:(hh + 1) * HDV], o_acc[:, osl], AF.Square,
                                 accum=stat[:, hh:hh + 1], waits=[tok_oacc],
                                 inc=(hh == H - 1), chain=(hh == 0))
                tv1 = VTS(stat[:, 4:8], stat[:, 0:4], 1.0 / HDV, EPS, ALU.mult, ALU.add,
                          waits=[tok_st], inc=True)
                ACT(stat[:, 4:8], stat[:, 4:8], AF.Ln, waits=[tv1])
                tok_rs = ACT(stat[:, 8:12], stat[:, 4:8], AF.Exp, scale=-0.5, inc=True)
                nwrep = normw_bc[:, :].unsqueeze(1).broadcast_to([128, H, HDV])
                VTT(ep_f1[:], o_acc[:, ch * DV:(ch + 1) * DV], nwrep, ALU.mult,
                    waits=[tok_nw, tok_rs])
                tok_a = None
                for hh in range(H):
                    ssl = slice(ch * DV + hh * HDV, ch * DV + hh * HDV + HDV)
                    tok_a = VSTT(silu_sp[:, ssl],
                                 ep_f1[:, hh * HDV:(hh + 1) * HDV],
                                 stat[:, 8 + hh:9 + hh], silu_sp[:, ssl], ALU.mult, ALU.mult,
                                 waits=[tok_silu],
                                 inc=(hh == H - 1), chain=(hh == 0))
                gate_tok[ch] = tok_a
            s1_tok = {}
            for ch in range(CPS):
                gchunk = s * CPS + ch
                xr = scr_f[:, (ch % 2) * D:(ch % 2) * D + D]
                tok_xr = DMA("sync", xr, x_h[gchunk * 128:(gchunk + 1) * 128, :],
                             dsem["s_xr"], waits=[xr_war.get(ch % 2), tok_emb])
                for g4 in range(2):
                    tk = None
                    for dc4 in range(4):
                        dc = g4 * 4 + dc4
                        tk = TP(pbb(7)[:, dc4 * 128:(dc4 + 1) * 128],
                                silu_sp[:, ch * DV + dc * 128:ch * DV + (dc + 1) * 128],
                                ident[:],
                                waits=[gate_tok[ch]] + (use_banks(7) if dc4 == 0 else []),
                                inc=(dc4 == 3))
                    ta_ = ACT(afm[:, g4 * 512:(g4 + 1) * 512], pbb(7)[:, 0:512],
                              AF.Copy, waits=[tk, afm_war.get(0)], inc=True,
                              chain=False)
                    done_banks(ta_, 7)
                    if g4 == 1 and ch == CPS - 1:
                        sil_war[0] = tk
                tok_afm = (sc, sc.v)
                ob0 = 0 if ch % 2 == 0 else 5
                tk_y = None
                for half in range(2):
                    for dc in range(DC):
                        tk_y = MM(pbf(ob0 + half)[:, 0:512], afm[:, dc * 128:(dc + 1) * 128],
                                  wvbuf[:, dc * D + half * 512: dc * D + half * 512 + 512],
                                  dc == 0, dc == DC - 1,
                                  waits=[tok_afm, tok_wo]
                                        + (use_banks(ob0 + half) if dc == 0 else []),
                                  inc=(half == 1 and dc == DC - 1))
                afm_war[0] = tk_y
                x1b = o_acc[:, ch * DV:(ch + 1) * DV]
                tok_s1 = None
                for half in range(2):
                    tok_s1 = VTT(x1b[:, half * 512:(half + 1) * 512],
                                 xr[:, half * 512:(half + 1) * 512],
                                 pbf(ob0 + half)[:, 0:512],
                                 ALU.add, waits=[tk_y, tok_xr], inc=(half == 1))
                    done_banks(tok_s1, ob0 + half)
                xr_war[ch % 2] = tok_s1
                s1_tok[ch] = tok_s1
            for ch in range(CPS):
                gchunk = s * CPS + ch
                x1b = o_acc[:, ch * DV:(ch + 1) * DV]
                ACT(ep_f1[:], x1b, AF.Identity, accum=stat[:, 12:13], waits=[s1_tok[ch]])
                tok_ms = ACT(ep_f1[:], x1b, AF.Square, accum=stat[:, 13:14], inc=True)
                VTS(stat[:, 14:15], stat[:, 12:13], 1.0 / D, None, ALU.mult, waits=[tok_ms])
                VTT(stat[:, 15:16], stat[:, 14:15], stat[:, 14:15], ALU.mult)
                VSTT(stat[:, 16:17], stat[:, 13:14], 1.0 / D, stat[:, 15:16],
                     ALU.mult, ALU.subtract)
                tv2 = VTS(stat[:, 17:18], stat[:, 16:17], EPS, None, ALU.add, inc=True)
                ACT(stat[:, 17:18], stat[:, 17:18], AF.Ln, waits=[tv2])
                tok_r1 = ACT(stat[:, 18:19], stat[:, 17:18], AF.Exp, scale=-0.5, inc=True)
                VTS(stat[:, 19:20], stat[:, 14:15], -1.0, None, ALU.mult, waits=[tok_r1])
                VTS(x1b, x1b, stat[:, 19:20], None, ALU.add)
                tok_x1 = VTS(x1b, x1b, stat[:, 18:19], None, ALU.mult, inc=True)
                tok_x1d = DMA("gpsimd", x1_hbm[gchunk * 128:(gchunk + 1) * 128, :], x1b,
                              dsem["s_x1w"], waits=[tok_x1])
                VRED(stat[:, 20:21], x1b, ALU.max, absval=True, waits=[tok_x1])
                VTS(stat[:, 21:22], stat[:, 16:17], EPS, None, ALU.add)
                VRCP(stat[:, 22:23], stat[:, 21:22])
                VTT(stat[:, 23:24], stat[:, 16:17], stat[:, 22:23], ALU.mult)
                tv3 = VTS(stat[:, 23:24], stat[:, 23:24], EPS, None, ALU.add, inc=True)
                ACT(stat[:, 23:24], stat[:, 23:24], AF.Ln, waits=[tv3])
                tok_rq = ACT(stat[:, 24:25], stat[:, 23:24], AF.Exp, scale=-0.5, inc=True)
                VTT(stat[:, 25:26], stat[:, 20:21], stat[:, 24:25], ALU.mult, waits=[tok_rq])
                VTS(stat[:, 25:26], stat[:, 25:26], 1e-5, None, ALU.max)
                VTS(stat[:, 27:28], stat[:, 25:26], 1.0 / 127.0, None, ALU.mult)
                VRCP(stat[:, 26:27], stat[:, 25:26])
                VTT(stat[:, 26:27], stat[:, 26:27], stat[:, 24:25], ALU.mult)
                VTS(stat[:, 26:27], stat[:, 26:27], 127.0, None, ALU.mult)
                if not _SIM_BUILD:
                    VTS(ep_i8[:], x1b, stat[:, 26:27], None, ALU.mult)
                    VCPY(ep_b1[:], ep_i8[:],
                         waits=[epb1_war.get(0), store_war.get(0)])
                else:
                    for qh in range(2):
                        qs = slice(qh * 512, (qh + 1) * 512)
                        zb = ep_f1[:, 0:512]
                        rb = ep_f1[:, 512:1024]
                        VTS(zb, x1b[:, qs], stat[:, 26:27], None, ALU.mult)
                        VTS(rb, zb, 1.0, None, ALU.mod)
                        E("vector", lambda eo: eo.scalar_tensor_tensor(
                            out=rb, in0=rb, scalar=0.5, in1=rb,
                            op0=ALU.is_ge, op1=ALU.subtract))
                        VTT(ep_b1[:, qs], zb, rb, ALU.add,
                            waits=([epb1_war.get(0), store_war.get(0)]
                                   if qh == 0 else ()))
                tok_q8 = VTS(ep_b1[:], ep_b1[:], stat[:, 27:28], None, ALU.mult, inc=True)
                oacc_free[ch] = [tok_q8, tok_x1d]
                for g4 in range(2):
                    tk = None
                    for dc4 in range(4):
                        dc = g4 * 4 + dc4
                        tk = TP(pbb(7)[:, dc4 * 128:(dc4 + 1) * 128],
                                ep_b1[:, dc * 128:(dc + 1) * 128], ident[:],
                                waits=[tok_q8] + (use_banks(7) if dc4 == 0 else []),
                                inc=(dc4 == 3))
                    tq8 = VCPY(ep_b1[:, g4 * 512:(g4 + 1) * 512], pbb(7)[:, 0:512],
                               waits=[tk], inc=True)
                    done_banks(tq8, 7)
                    if g4 == 1:
                        epb1_war[0] = tk
                tok_x1q = DMA(
                    "gpsimd",
                    x1q_hbm.rearrange("(c p) t -> p c t", p=128)[
                        :, :, gchunk * 128:(gchunk + 1) * 128],
                    ep_b1[:].rearrange("p (c t) -> p c t", c=DC), dsem["s_x1qw"],
                    waits=[(vc, vc.v)])
                store_war[0] = tok_x1q
        join()

        if phases < 3:
            _finish(nc, P, so)
            return nc

        if phases < 4:
            _finish(nc, P, so)
            return nc
        for wi in range(1):
            rtoks = {}
            for cch in range(32):
                par = cch % 4
                dst = fwbuf[:, par * D:(par + 1) * D]
                src = w1_h[cch * 128:(cch + 1) * 128, :]
                qn = ("s_fw", "s_x", "s_xr")[cch % 3]
                tok_l = DMA("sync", dst, src, dsem[qn], waits=[rtoks.get(par)])
                rtoks[par] = VRED(wred[:, cch:cch + 1], dst, ALU.add, absval=True,
                                  waits=[tok_l], inc=True)
            tok_t = VRED(wred[:, 32:33], wred[:, 0:32], ALU.add, waits=[(vc, vc.v)],
                         inc=True)
            tk = MM(pbf(2)[0:1, 0:1], wred[:, 32:33], ones_col[:], True, True,
                    waits=[tok_t] + use_banks(2), inc=True)
            tok_m1 = VTS(stat[0:1, 0:1], pbf(2)[0:1, 0:1], 1.0 / float(FF * D), None,
                         ALU.mult, waits=[tk], inc=True)
            done_banks(tok_m1, 2)
            VTS(stat[0:1, 1:2], stat[0:1, 0:1], 1e-5, None, ALU.max)
            tok_m2 = VRCP(stat[0:1, 2:3], stat[0:1, 1:2], inc=True)
            MM(pbf(2)[:, 0:1], ones_f[:], stat[0:1, 2:3], True, True,
               waits=[tok_m2] + use_banks(2))
            tk = MM(pbf(2)[:, 1:2], ones_f[:], stat[0:1, 1:2], True, True, inc=True)
            tbc = VCPY(scalew[:, 2 * wi:2 * wi + 2], pbf(2)[:, 0:2], waits=[tk], inc=True)
            done_banks(tbc, 2)

        qtoks = {}
        for cch in range(32):
            par = cch % 4
            fb = fwbuf[:, par * D:(par + 1) * D]
            qn = ("s_fw", "s_x", "s_xr")[cch % 3]
            tok_l = DMA("sync", fb, w1_h[cch * 128:(cch + 1) * 128, :], dsem[qn],
                        waits=[qtoks.get(par), rtoks.get(par)])
            if not _SIM_BUILD:
                VTS(fqi8[:], fb, scalew[:, 0:1], None, ALU.mult, waits=[tok_l, (vc, vc.v)])
                VTS(fqbf[:], fqi8[:], 1.0, -1.0, ALU.min, ALU.max)
            else:
                VTS(simz[:], fb, scalew[:, 0:1], None, ALU.mult, waits=[tok_l, (vc, vc.v)])
                VTS(simr[:], simz[:], 1.0, None, ALU.mod)
                E("vector", lambda eo: eo.scalar_tensor_tensor(
                    out=simr[:], in0=simr[:], scalar=0.5, in1=simr[:],
                    op0=ALU.is_ge, op1=ALU.subtract))
                VTT(simz[:], simz[:], simr[:], ALU.add)
                VTS(fqbf[:], simz[:], 1.0, -1.0, ALU.min, ALU.max)
            tok_qn = VTS(fqbf[:], fqbf[:], scalew[:, 1:2], None, ALU.mult, inc=True)
            for g4 in range(2):
                tk = None
                for dc4 in range(4):
                    dc = g4 * 4 + dc4
                    tk = TP(pbb(7)[:, dc4 * 128:(dc4 + 1) * 128],
                            fqbf[:, dc * 128:(dc + 1) * 128], ident[:],
                            waits=[tok_qn] + (use_banks(7) if dc4 == 0 else []),
                            inc=(dc4 == 3))
                dst = w1T[:].rearrange("p (c f) -> p c f", c=DC)[
                    :, g4 * 4:(g4 + 1) * 4, cch * 128:(cch + 1) * 128]
                tw = VCPY(dst, pbb(7)[:, 0:512].rearrange("p (a b) -> p a b", a=4),
                          waits=[tk], inc=True)
                done_banks(tw, 7)
                if g4 == 1:
                    qtoks[par] = tk
        tok_wT = (vc, vc.v)
        join()

        if phases < 5:
            _finish(nc, P, so)
            return nc
        ftoks = {}
        f1_hq_war = {}
        f1_st_war = {}
        w2rtoks = {}
        for tt in range(NTILE):
            par = tt % 2
            xq = x1q_t[:, par * D:(par + 1) * D]
            tok_xq = DMA("sync", xq.rearrange("p (c t) -> p c t", c=DC),
                         x1q_hbm.rearrange("(c p) t -> p c t", p=128)[
                             :, :, tt * 128:(tt + 1) * 128], dsem["s_x1qr"],
                         waits=[ftoks.get(par)])
            for fc in range(8):
                bk = fc % 2
                for dc in range(DC):
                    MM(pbf(bk)[:, 0:512], xq[:, dc * 128:(dc + 1) * 128],
                       w1T[:, dc * FF + fc * 512: dc * FF + fc * 512 + 512],
                       dc == 0, False,
                       waits=[tok_xq, tok_wT] + (use_banks(bk) if dc == 0 else []))
                tk = MM(pbf(bk)[:, 0:512], ones_b[:], b1row[:, fc * 512:(fc + 1) * 512],
                        False, True, inc=True)
                if fc == 7:
                    ftoks[par] = tk
                th = ACT(h_bf[:, fc * 512:(fc + 1) * 512], pbf(bk)[:, 0:512], AF.Gelu,
                         waits=[tk] + ([f1_hq_war.get(0)] if fc == 0 else []),
                         inc=True, chain=(fc == 0))
                done_banks(th, bk)
            tok_h = (sc, sc.v)
            if debug and tt == 0:
                DMA("sync", dbg["h0"][:, :], h_bf[:], dsem["s_attnw"], waits=[tok_h])
                DMA("sync", dbg["w1T"][:, :], w1T[:], dsem["s_attnw"], waits=[tok_h])
            tok_am = VRED(stat[:, 0:1], h_bf[:], ALU.max, absval=True,
                          waits=[tok_h, (sc, sc.v)], inc=True)
            tok_sq2 = ACT(hsq[:], h_bf[:], AF.Square, accum=stat[:, 1:2],
                          waits=[tok_h, f1_st_war.get(0)], inc=True)
            VCPY(fstash[:, 2 * tt:2 * tt + 1], stat[:, 0:1], waits=[tok_am, tok_sq2])
            VCPY(fstash[:, 2 * tt + 1:2 * tt + 2], stat[:, 1:2])
            VTS(stat[:, 2:3], stat[:, 0:1], 1e-20, None, ALU.max)
            VRCP(stat[:, 3:4], stat[:, 2:3])
            tok_cq = VTS(stat[:, 3:4], stat[:, 3:4], 127.0, None, ALU.mult, inc=True)
            if not _SIM_BUILD:
                VTS(hq_i8[:], h_bf[:], stat[:, 3:4], None, ALU.mult, waits=[tok_cq])
                tok_hq = VCPY(hsq[:], hq_i8[:], inc=True)
            else:
                for qc in range(4):
                    qs = slice(qc * D, (qc + 1) * D)
                    VTS(simz[:], h_bf[:, qs], stat[:, 3:4], None, ALU.mult,
                        waits=([tok_cq] if qc == 0 else ()))
                    VTS(simr[:], simz[:], 1.0, None, ALU.mod)
                    E("vector", lambda eo: eo.scalar_tensor_tensor(
                        out=simr[:], in0=simr[:], scalar=0.5, in1=simr[:],
                        op0=ALU.is_ge, op1=ALU.subtract))
                    VTT(hsq[:, qs], simz[:], simr[:], ALU.add)
                tok_hq = VNOP()
            f1_hq_war[0] = tok_hq
            f1_st_war[0] = DMA("gpsimd", hq_hbm[tt * 128:(tt + 1) * 128, :], hsq[:],
                               dsem["s_m2w"], waits=[tok_hq])
            for c2 in (2 * tt, 2 * tt + 1):
                par2 = c2 % 2
                dj2, aa2 = c2 // 4, c2 % 4
                dst2 = f2wbuf[:, par2 * D:(par2 + 1) * D]
                tok_l2 = DMA("sync", dst2,
                             w2_h[dj2 * 128:(dj2 + 1) * 128, aa2 * D:(aa2 + 1) * D],
                             dsem["s_fw"], waits=[w2rtoks.get(par2)])
                w2rtoks[par2] = VRED(wred[:, c2:c2 + 1], dst2, ALU.add, absval=True,
                                     waits=[tok_l2], inc=True)
        if debug:
            DMA("sync", dbg["fstash"][:, :], fstash[:], dsem["s_attnw"], waits=[(vc, vc.v)])
        join()

        rtoks = w2rtoks
        tok_t = VRED(wred[:, 32:33], wred[:, 0:32], ALU.add, waits=[(vc, vc.v)],
                     inc=True)
        tk = MM(pbf(2)[0:1, 0:1], wred[:, 32:33], ones_col[:], True, True,
                waits=[tok_t] + use_banks(2), inc=True)
        tok_m1 = VTS(stat[0:1, 0:1], pbf(2)[0:1, 0:1], 1.0 / float(FF * D), None,
                     ALU.mult, waits=[tk], inc=True)
        done_banks(tok_m1, 2)
        VTS(stat[0:1, 1:2], stat[0:1, 0:1], 1e-5, None, ALU.max)
        tok_m2s = VRCP(stat[0:1, 2:3], stat[0:1, 1:2], inc=True)
        MM(pbf(2)[:, 0:1], ones_f[:], stat[0:1, 2:3], True, True,
           waits=[tok_m2s] + use_banks(2))
        tk = MM(pbf(2)[:, 1:2], ones_f[:], stat[0:1, 1:2], True, True, inc=True)
        tbc = VCPY(scalew[:, 2:4], pbf(2)[:, 0:2], waits=[tk], inc=True)
        done_banks(tbc, 2)
        qtoks = {}
        for cch in range(32):
            par = cch % 2
            dj, aa = cch // 4, cch % 4
            fb = f2wbuf[:, par * D:(par + 1) * D]
            tok_l = DMA("sync", fb, w2_h[dj * 128:(dj + 1) * 128, aa * D:(aa + 1) * D],
                        dsem["s_fw"], waits=[qtoks.get(par)])
            if not _SIM_BUILD:
                VTS(f2qi8[:], fb, scalew[:, 2:3], None, ALU.mult,
                    waits=[tok_l, (vc, vc.v)])
                tok_qn = VTS(f2qbf[:], f2qi8[:], 1.0, -1.0, ALU.min, ALU.max, inc=True)
            else:
                VTS(simz2[:], fb, scalew[:, 2:3], None, ALU.mult,
                    waits=[tok_l, (vc, vc.v)])
                VTS(simr2[:], simz2[:], 1.0, None, ALU.mod)
                E("vector", lambda eo: eo.scalar_tensor_tensor(
                    out=simr2[:], in0=simr2[:], scalar=0.5, in1=simr2[:],
                    op0=ALU.is_ge, op1=ALU.subtract))
                VTT(simz2[:], simz2[:], simr2[:], ALU.add)
                tok_qn = VTS(f2qbf[:], simz2[:], 1.0, -1.0, ALU.min, ALU.max, inc=True)
            for g4 in range(2):
                tk = None
                for fs4 in range(4):
                    fs = g4 * 4 + fs4
                    tk = TP(pbb(7)[:, fs4 * 128:(fs4 + 1) * 128],
                            f2qbf[:, fs * 128:(fs + 1) * 128], ident[:],
                            waits=[tok_qn] + (use_banks(7) if fs4 == 0 else []),
                            inc=(fs4 == 3))
                dst = w2T[:].rearrange("p (c f) -> p c f", c=FF // 128)[
                    :, aa * 8 + g4 * 4: aa * 8 + (g4 + 1) * 4, dj * 128:(dj + 1) * 128]
                tw = VCPY(dst, pbb(7)[:, 0:512].rearrange("p (a b) -> p a b", a=4),
                          waits=[tk], inc=True)
                done_banks(tw, 7)
                if g4 == 1:
                    qtoks[par] = tk
        tok_wT2 = (vc, vc.v)
        fintoks = {}
        hq2war = {}
        f2_st_war = {}
        if debug:
            DMA("sync", dbg["scalew"][:, :], scalew[:], dsem["s_attnw"], waits=[(vc, vc.v)])
        join()
        def emit_final(ptt, ppar, ptm2):
            m2v = m2_t[:, ppar * D:(ppar + 1) * D]
            x1v = fin_x1b[:, ppar * D:(ppar + 1) * D]
            tok_x = DMA("sync", x1v, x1_hbm[ptt * 128:(ptt + 1) * 128, :], dsem["s_x1r"],
                        waits=[fintoks.get(ppar)])
            tv = VTS(stat[:, 0:1], fstash[:, 2 * ptt + 1:2 * ptt + 2], 1.0 / FF, EPS,
                     ALU.mult, ALU.add, inc=True)
            ACT(stat[:, 0:1], stat[:, 0:1], AF.Ln, waits=[tv])
            tok_rq = ACT(stat[:, 1:2], stat[:, 0:1], AF.Exp, scale=-0.5, inc=True)
            VTT(stat[:, 2:3], fstash[:, 2 * ptt:2 * ptt + 1], stat[:, 1:2], ALU.mult,
                waits=[tok_rq])
            VTS(stat[:, 2:3], stat[:, 2:3], 1e-5, None, ALU.max)
            VTT(stat[:, 2:3], stat[:, 2:3], scalew[:, 3:4], ALU.mult)
            VTS(stat[:, 2:3], stat[:, 2:3], 1.0 / 127.0, None, ALU.mult)
            VTS(m2v, m2v, stat[:, 2:3], None, ALU.mult, waits=[ptm2])
            VTT(m2v, m2v, b2bc[:], ALU.add)
            tok_s2 = VTT(m2v, m2v, x1v, ALU.add, waits=[tok_x], inc=True)
            ACT(fin_ob[:], m2v, AF.Identity, accum=stat[:, 4:5],
                waits=[tok_s2, f2_st_war.get(0)])
            tok_ms = ACT(fin_ob[:], m2v, AF.Square, accum=stat[:, 5:6], inc=True)
            VTS(stat[:, 6:7], stat[:, 4:5], 1.0 / D, None, ALU.mult, waits=[tok_ms])
            VTT(stat[:, 7:8], stat[:, 6:7], stat[:, 6:7], ALU.mult)
            VSTT(stat[:, 8:9], stat[:, 5:6], 1.0 / D, stat[:, 7:8], ALU.mult, ALU.subtract)
            tv2 = VTS(stat[:, 9:10], stat[:, 8:9], EPS, None, ALU.add, inc=True)
            ACT(stat[:, 9:10], stat[:, 9:10], AF.Ln, waits=[tv2])
            tok_r2 = ACT(stat[:, 10:11], stat[:, 9:10], AF.Exp, scale=-0.5, inc=True)
            VTS(stat[:, 11:12], stat[:, 6:7], -1.0, None, ALU.mult, waits=[tok_r2])
            VTS(fin_ob[:], m2v, stat[:, 11:12], None, ALU.add)
            tok_o = VTS(fin_ob[:], fin_ob[:], stat[:, 10:11], None, ALU.mult, inc=True)
            f2_st_war[0] = DMA("gpsimd", out_h[ptt * 128:(ptt + 1) * 128, :], fin_ob[:],
                               so, waits=[tok_o])
            fintoks[ppar] = tok_s2

        pend = None
        for tt in range(NTILE + 1):
            if tt < NTILE:
                par = tt % 2
                hqb = hq2_bf[:, par * FF:(par + 1) * FF]
                hfm = hq_fm[:, par * FF:(par + 1) * FF]
                tok_hb = DMA("sync", hqb, hq_hbm[tt * 128:(tt + 1) * 128, :],
                             dsem["s_x1qr"], waits=[hq2war.get(par)])
                for g8 in range(FF // 128 // 4):
                    tb = g8 % 2
                    tk = None
                    for f4 in range(4):
                        fi = g8 * 4 + f4
                        tk = MM(pbf(tb)[:, f4 * 128:(f4 + 1) * 128],
                                hqb[:, fi * 128:(fi + 1) * 128], ident[:],
                                True, True,
                                waits=[tok_hb] + (use_banks(tb) if f4 == 0 else []),
                                inc=(f4 == 3))
                    thf = ACT(hfm[:, g8 * 512:(g8 + 1) * 512], pbf(tb)[:, 0:512],
                              AF.Copy, waits=[tk], inc=True, chain=False)
                    done_banks(thf, tb)
                tok_hqf = (sc, sc.v)
                hq2war[par] = tok_hqf
                tk_y2 = None
                b0 = 2 + 2 * par
                for half in range(2):
                    bk = b0 + half
                    for fi in range(FF // 128):
                        tk_y2 = MM(pbf(bk)[:, 0:512], hfm[:, fi * 128:(fi + 1) * 128],
                                   w2T[:, fi * D + half * 512: fi * D + half * 512 + 512],
                                   fi == 0, fi == FF // 128 - 1,
                                   waits=[tok_hqf, tok_wT2] + (use_banks(bk) if fi == 0 else []),
                                   inc=(half == 1 and fi == FF // 128 - 1))
                m2v_ = m2_t[:, par * D:(par + 1) * D]
                ACT(m2v_[:, 0:512], pbf(b0)[:, 0:512], AF.Copy, waits=[tk_y2],
                    chain=False)
                tm2 = ACT(m2v_[:, 512:1024], pbf(b0 + 1)[:, 0:512], AF.Copy, inc=True)
                done_banks(tm2, b0, b0 + 1)
                this = (tt, par, tm2)
            else:
                this = None
            if pend is not None:
                emit_final(*pend)
            pend = this

        _finish(nc, P, so)
    return nc


def _finish(nc, P, so):
    for eng in Prog.ENGINES:
        if eng == "sync":
            continue
        P.emit(eng, lambda eo: None,
               waits=[(s, s.v) for s in []])
    P.emit("sync", lambda eo: None, waits=[(so, so.v)])
    with nc.Block() as block:
        @block.tensor
        def _(eo):
            for f in P.streams["tensor"]:
                f(eo)

        @block.vector
        def _(eo):
            for f in P.streams["vector"]:
                f(eo)

        @block.scalar
        def _(eo):
            for f in P.streams["scalar"]:
                f(eo)

        @block.gpsimd
        def _(eo):
            for f in P.streams["gpsimd"]:
                f(eo)

        @block.sync
        def _(eo):
            for f in P.streams["sync"]:
                f(eo)


_IN_ORDER = [
    ("x", "L0_attn_x"), ("w_gate", "L0_attn_w_gate"), ("w_q", "L0_attn_w_q"),
    ("w_k", "L0_attn_w_k"), ("w_v", "L0_attn_w_v"), ("w_gk1", "L0_attn_w_gk1"),
    ("w_gk2", "L0_attn_w_gk2"), ("b_gk", "L0_attn_b_gk"), ("w_ks", "L0_attn_w_ks"),
    ("w_vs", "L0_attn_w_vs"), ("w_gk1s", "L0_attn_w_gk1s"), ("w_gk2s", "L0_attn_w_gk2s"),
    ("b_gks", "L0_attn_b_gks"), ("w_g", "L0_attn_w_g"), ("w_o", "L0_attn_w_o"),
    ("norm_w", "L0_attn_norm_w"), ("ffn_w1", "L0_ffn_w1"), ("ffn_b1", "L0_ffn_b1"),
    ("ffn_w2", "L0_ffn_w2"), ("ffn_b2", "L0_ffn_b2"),
]


def kernel(**inputs):
    nc = build_nc(debug=False)
    base = {}
    for dev_name, in_name in _IN_ORDER:
        base[dev_name] = np.ascontiguousarray(np.asarray(inputs[in_name]))
    x = base.pop("x")
    in_maps = []
    for i in range(8):
        d = dict(base)
        d["x"] = np.ascontiguousarray(x[i])
        in_maps.append(d)
    res = run_bass_kernel_spmd(nc, in_maps, core_ids=list(range(8)))
    out = np.stack([res.results[i]["out"] for i in range(8)], axis=0)
    return out.astype(np.float32)


# version 31
# speedup vs baseline: 1.0285x; 1.0285x over previous
import contextlib
import os
import numpy as np

_SIM_BUILD = os.environ.get("BASS_SIM_FP64") == "1"
_A_CUT = int(os.environ.get("A_CUT", "99"))
_NSPAN_LIM = int(os.environ.get("NSPAN_LIM", "99"))
_G_CUT = int(os.environ.get("G_CUT", "99"))

import concourse.bass as bass
import concourse.mybir as mybir
from concourse.bass_utils import run_bass_kernel_spmd

F32 = mybir.dt.float32
BF16 = mybir.dt.bfloat16
I8 = mybir.dt.int8
AF = mybir.ActivationFunctionType
ALU = mybir.AluOpType
AX = mybir.AxisListType

T = 2048
D = 1024
DK = 512
DV = 1024
H = 4
HDK = 128
HDV = 256
M = 4
CAP = 1024
FF = 4096
SPAN = 512
NSPAN = T // SPAN
C = 128
CPS = SPAN // C
DC = D // 128
NPATH = 5
GN = 16.0
EPS = 1e-5
QSCALE = float(HDK) ** -0.5
NTILE = T // 128


class Sem:
    def __init__(self, h):
        self.h = h
        self.v = 0


class Prog:
    ENGINES = ("tensor", "vector", "scalar", "gpsimd", "sync")

    def __init__(self):
        self.streams = {e: [] for e in self.ENGINES}
        self.last_wait = {e: {} for e in self.ENGINES}
        self.maxw = {}
        self.auto_sems = {}
        self.prev_tok = {}
        self.sim_barrier = []

    def emit(self, eng, fn, waits=(), incs=(), is_dma=False, chain=True):
        if _SIM_BUILD and self.sim_barrier:
            waits = list(waits) + [(s, s.v) for s in self.sim_barrier]
        auto = self.auto_sems.get(eng)
        incs = list(incs)
        if auto is not None:
            if chain:
                prev = self.prev_tok.get(eng)
                if prev is not None:
                    waits = list(waits) + [prev]
            if not is_dma and not incs and chain:
                incs.append((auto, 1))
        ws = []
        for w in waits:
            if w is None:
                continue
            sem, val = w
            if val is None or val <= 0:
                continue
            if val > self.maxw.get(sem, 0):
                self.maxw[sem] = val
            if val > self.last_wait[eng].get(sem, 0):
                ws.append((sem, val))
                self.last_wait[eng][sem] = val
        toks = []
        inc_list = []
        for sem, n in incs:
            sem.v += n
            inc_list.append((sem, n))
            toks.append((sem, sem.v))
        def closure(eo, ws=ws, fn=fn, inc_list=inc_list):
            for sem, val in ws:
                eo.wait_ge(sem.h, val)
            op = fn(eo)
            for sem, n in inc_list:
                if op is not None:
                    op = op.then_inc(sem.h, n)
                else:
                    eo.sem_inc(sem.h, n)
        self.streams[eng].append(closure)
        if auto is not None and not is_dma and toks:
            self.prev_tok[eng] = toks[0]
        ret = [t for t in toks if t[0] is not auto] or toks
        return ret[0] if len(ret) == 1 else ret


class Pool:

    def __init__(self, t32, n32, t16, n16, t8, n8):
        self.t = {F32: t32, BF16: t16, I8: t8}
        self.n = {F32: n32, BF16: n16, I8: n8}
        self.off = {F32: 0, BF16: 0, I8: 0}

    def take(self, nelem, dtype, parts=128):
        isz = {F32: 4, BF16: 2, I8: 1}[dtype]
        al = max(1, 64 // isz)
        self.off[dtype] = (self.off[dtype] + al - 1) // al * al
        start = self.off[dtype]
        self.off[dtype] += nelem
        assert self.off[dtype] <= self.n[dtype], (
            f"pool overflow {dtype}: {self.off[dtype]} > {self.n[dtype]}")
        v = self.t[dtype][:, start:start + nelem]
        if parts != 128:
            v = v[0:parts]
        return v

    def f32(self, cols, parts=128):
        return self.take(cols, F32, parts)

    def bf16(self, cols, parts=128):
        return self.take(cols, BF16, parts)

    def i8(self, cols, parts=128):
        return self.take(cols, I8, parts)

    def fork(self):
        p = Pool(self.t[F32], self.n[F32], self.t[BF16], self.n[BF16],
                 self.t[I8], self.n[I8])
        p.off = dict(self.off)
        return p


def build_nc(debug=False, phases=5):
    nc = bass.Bass("TRN2", target_bir_lowering=False, detect_race_conditions=False)

    x_h = nc.dram_tensor("x", [T, D], F32, kind="ExternalInput").ap()
    wgate_h = nc.dram_tensor("w_gate", [D, M], F32, kind="ExternalInput").ap()
    wq_h = nc.dram_tensor("w_q", [D, DK], F32, kind="ExternalInput").ap()
    wk_h = nc.dram_tensor("w_k", [M, D, DK], F32, kind="ExternalInput").ap()
    wv_h = nc.dram_tensor("w_v", [M, D, DV], F32, kind="ExternalInput").ap()
    wgk1_h = nc.dram_tensor("w_gk1", [M, D, 16], F32, kind="ExternalInput").ap()
    wgk2_h = nc.dram_tensor("w_gk2", [M, 16, DK], F32, kind="ExternalInput").ap()
    bgk_h = nc.dram_tensor("b_gk", [M, DK], F32, kind="ExternalInput").ap()
    wks_h = nc.dram_tensor("w_ks", [D, DK], F32, kind="ExternalInput").ap()
    wvs_h = nc.dram_tensor("w_vs", [D, DV], F32, kind="ExternalInput").ap()
    wgk1s_h = nc.dram_tensor("w_gk1s", [D, 16], F32, kind="ExternalInput").ap()
    wgk2s_h = nc.dram_tensor("w_gk2s", [16, DK], F32, kind="ExternalInput").ap()
    bgks_h = nc.dram_tensor("b_gks", [DK], F32, kind="ExternalInput").ap()
    wg_h = nc.dram_tensor("w_g", [D, DV], F32, kind="ExternalInput").ap()
    wo_h = nc.dram_tensor("w_o", [DV, D], F32, kind="ExternalInput").ap()
    normw_h = nc.dram_tensor("norm_w", [HDV], F32, kind="ExternalInput").ap()
    w1_h = nc.dram_tensor("ffn_w1", [FF, D], F32, kind="ExternalInput").ap()
    b1_h = nc.dram_tensor("ffn_b1", [FF], F32, kind="ExternalInput").ap()
    w2_h = nc.dram_tensor("ffn_w2", [D, FF], F32, kind="ExternalInput").ap()
    b2_h = nc.dram_tensor("ffn_b2", [D], F32, kind="ExternalInput").ap()
    out_h = nc.dram_tensor("out", [T, D], F32, kind="ExternalOutput").ap()

    scr_kind = "ExternalOutput" if debug else "Internal"
    attn_hbm = nc.dram_tensor("attn_scr", [T, DV], BF16, kind=scr_kind).ap()
    x1_hbm = nc.dram_tensor("x1_scr", [T, D], F32, kind=scr_kind).ap()
    x1q_hbm = nc.dram_tensor("x1q_scr", [D, T], BF16, kind=scr_kind).ap()
    m2_hbm = nc.dram_tensor("m2_scr", [T, D], F32, kind=scr_kind).ap()
    hq_hbm = nc.dram_tensor("hq_scr", [T, FF], BF16, kind="Internal").ap()
    dbg = {}
    if debug:
        dbg["route"] = nc.dram_tensor("dbg_route", [4, M, T], F32,
                                      kind="ExternalOutput").ap()
        dbg["fstash"] = nc.dram_tensor("dbg_fstash", [128, 2 * NTILE], F32,
                                       kind="ExternalOutput").ap()
        dbg["scalew"] = nc.dram_tensor("dbg_scalew", [128, 4], F32,
                                       kind="ExternalOutput").ap()
        dbg["w1T"] = nc.dram_tensor("dbg_w1T", [128, DC * FF], BF16,
                                    kind="ExternalOutput").ap()
        dbg["h0"] = nc.dram_tensor("dbg_h0", [128, FF], BF16,
                                   kind="ExternalOutput").ap()

    es = contextlib.ExitStack()
    with es:
        N32 = 12992
        N16 = 77664
        N8 = 5120
        p32 = es.enter_context(nc.sbuf_tensor("p32", [128, N32], F32))
        p16 = es.enter_context(nc.sbuf_tensor("p16", [128, N16], BF16))
        p8 = es.enter_context(nc.sbuf_tensor("p8", [128, N8], I8))
        G = Pool(p32, N32, p16, N16, p8, N8)

        ident = G.bf16(128)
        identf = G.f32(128)
        trimask = G.bf16(128)
        ones_f = G.f32(128, parts=1)
        ones_b = G.bf16(128, parts=1)
        ones_col = G.f32(1)
        normw_row = G.f32(HDV, parts=1)
        normw_bc = G.bf16(HDV)
        wgate_l = G.f32(DC * M)
        bgk_sb = G.f32(NPATH * H)
        gkw1 = G.bf16(NPATH * DC * 16)
        gkw2 = G.bf16(NPATH * DK, parts=16)
        stat = G.f32(32)
        weff = G.bf16(T, parts=M)
        kept = G.bf16(T, parts=M)
        b1row = G.bf16(FF, parts=1)
        b2row = G.f32(D, parts=1)
        b2bc = G.f32(D)
        scalew = G.f32(4)
        x_tm = G.f32(2 * D)
        x_tmb = G.bf16(2 * D)
        selm = G.bf16(M * 128, parts=M)
        fstash = G.f32(2 * NTILE)

        A = G.fork()
        sg5 = A.bf16(NPATH * H * SPAN)
        qspan = A.bf16(H * SPAN)
        xfm = A.bf16(DC * SPAN)
        vtil = A.bf16(CPS * DV)
        wBs = A.bf16(M * SPAN)
        rBs = A.bf16(M * SPAN)
        scr_f = A.f32(H * SPAN)
        scr_b1 = A.bf16(H * SPAN)
        scr_b2 = A.bf16(H * SPAN)
        qw = A.bf16(H * SPAN)
        ktil = A.bf16(H * SPAN)
        khat = A.bf16(H * C)
        khat_tm = A.bf16(H * C)
        ebc5 = A.f32(NPATH * H * CPS)
        gklow = A.bf16(CPS * 16)
        gklow_slab = A.bf16(SPAN)
        gklow_fm = gklow_slab[0:16]
        wkbuf = A.bf16(DC * DK)
        wvbuf = A.bf16(DC * DV)
        o_acc = A.f32(CPS * DV)
        silu_sp = A.bf16(CPS * DV)
        S0 = A.bf16(NPATH * H * HDV)
        S1 = A.bf16(NPATH * H * HDV)
        ep_f1 = A.f32(DV)
        ep_b1 = A.bf16(DV)
        ep_b2x = A.bf16(DV)
        afm = ep_b2x
        ep_i8 = A.i8(D)

        R_ = G.fork()
        lg_fm = R_.f32(T, parts=M)
        pos_fm = R_.f32(T, parts=M)
        top2_fm = R_.bf16(T, parts=M)
        wpre_fm = R_.bf16(T, parts=M)
        rt_tm = R_.f32(16)
        xfm32 = R_.f32(DC * SPAN)


        Fp = G.fork()
        w1T = Fp.bf16(DC * FF)
        fwbuf = Fp.f32(4 * D)
        fqi8 = Fp.i8(D)
        fqbf = Fp.bf16(D)
        h_bf = Fp.bf16(FF)
        hsq = Fp.bf16(FF)
        hq_i8 = Fp.i8(FF)
        x1q_t = Fp.bf16(2 * D)
        wred = Fp.f32(34)
        simz = Fp.f32(D) if _SIM_BUILD else None
        simr = fwbuf[:, D:2 * D] if _SIM_BUILD else None
        F2 = G.fork()
        w2T = F2.bf16((FF // 128) * D)
        f2wbuf = F2.f32(2 * D)
        f2qi8 = F2.i8(D)
        f2qbf = F2.bf16(D)
        hq2_bf = F2.bf16(2 * FF)
        hq_fm = F2.bf16(2 * FF)
        m2_t = F2.f32(2 * D)
        fin_x1b = F2.f32(2 * D)
        fin_ob = F2.f32(D)
        simz2 = F2.f32(D) if _SIM_BUILD else None
        simr2 = f2wbuf[:, D:2 * D] if _SIM_BUILD else None

        L_ = G.fork()
        fin_m2 = L_.f32(2 * D)
        fin_x1 = L_.f32(2 * D)
        fin_s2 = L_.f32(D)
        fin_o = L_.f32(D)

        pbank = [es.enter_context(nc.psum_tensor(f"pb{i}", [128, 512], F32))
                 for i in range(7)]
        pbank.append(es.enter_context(nc.psum_tensor("pb7", [128, 1024], BF16)))

        def pbf(i):
            assert i < 7
            return pbank[i][:]

        def pbb(i):
            assert i == 7
            return pbank[7][:]

        tn = Sem(es.enter_context(nc.semaphore("tn")))
        vc = Sem(es.enter_context(nc.semaphore("vc")))
        sc = Sem(es.enter_context(nc.semaphore("sc")))
        gp = Sem(es.enter_context(nc.semaphore("gp")))
        so = Sem(es.enter_context(nc.semaphore("so")))
        _dma_names = ["sq", "gq", "s_x", "s_xr", "s_attnw", "s_attnr", "s_x1w",
                      "s_x1r", "s_x1qw", "s_x1qr", "s_m2w", "s_m2r", "s_fw",
                      "g_wk", "g_wv", "g_wq", "g_wg", "g_wo"]
        dsem = {n: Sem(es.enter_context(nc.semaphore(n))) for n in _dma_names}
        sq = dsem["sq"]
        gq = dsem["gq"]

        P = Prog()
        P.auto_sems = {"vector": vc, "scalar": sc, "gpsimd": gp}
        P.sim_barrier = [tn, vc, sc, gp]
        E = P.emit
        bank_tok = {}

        def use_banks(*ids):
            out = []
            for i in ids:
                out.extend(bank_tok.get(i, []))
            return out

        def done_banks(tok, *ids):
            for i in ids:
                bank_tok[i] = [tok]

        def also_banks(tok, *ids):
            for i in ids:
                bank_tok.setdefault(i, []).append(tok)

        def DMA(queue, out, in_, sem, waits=()):
            ws = list(waits) + [(sem, sem.v if _SIM_BUILD else P.maxw.get(sem, 0))]
            if _SIM_BUILD:
                ws += [(tn, tn.v), (vc, vc.v), (sc, sc.v), (gp, gp.v)]
            return E(queue, lambda eo, o=out, i=in_: eo.dma_start(out=o, in_=i),
                     waits=ws, incs=[(sem, 16)], is_dma=True)

        def MM(out, lhsT, rhs, start, stop, waits=(), inc=False):
            return E("tensor",
                     lambda eo, o=out, l=lhsT, r=rhs, a=start, b=stop:
                     eo.matmul(o, l, r, start=a, stop=b),
                     waits=waits, incs=([(tn, 1)] if inc else ()))

        def TP(out, in_, idm, waits=(), inc=False):
            return E("tensor",
                     lambda eo, o=out, i=in_, d=idm: eo.transpose(o, i, d),
                     waits=waits, incs=([(tn, 1)] if inc else ()))

        def TNOP(waits=()):
            return E("tensor", lambda eo: None, waits=waits, incs=[(tn, 1)])

        def VCPY(out, in_, waits=(), inc=False, chain=True):
            return E("vector", lambda eo, o=out, i=in_: eo.tensor_copy(out=o, in_=i),
                     waits=waits, incs=([(vc, 1)] if inc else ()), chain=chain)

        def VTT(out, in0, in1, op, waits=(), inc=False, chain=True):
            return E("vector",
                     lambda eo, o=out, a=in0, b=in1, p=op:
                     eo.tensor_tensor(out=o, in0=a, in1=b, op=p),
                     waits=waits, incs=([(vc, 1)] if inc else ()), chain=chain)

        def VTS(out, in0, s1, s2, op0, op1=None, waits=(), inc=False):
            if op1 is None:
                return E("vector",
                         lambda eo, o=out, a=in0, x=s1, p=op0:
                         eo.tensor_scalar(out=o, in0=a, scalar1=x, scalar2=None, op0=p),
                         waits=waits, incs=([(vc, 1)] if inc else ()))
            return E("vector",
                     lambda eo, o=out, a=in0, x=s1, y=s2, p=op0, q=op1:
                     eo.tensor_scalar(out=o, in0=a, scalar1=x, scalar2=y, op0=p, op1=q),
                     waits=waits, incs=([(vc, 1)] if inc else ()))

        def VSTT(out, in0, scalar, in1, op0, op1, waits=(), inc=False, chain=True):
            return E("vector",
                     lambda eo, o=out, a=in0, s=scalar, b=in1, p=op0, q=op1:
                     eo.scalar_tensor_tensor(out=o, in0=a, scalar=s, in1=b, op0=p, op1=q),
                     waits=waits, incs=([(vc, 1)] if inc else ()), chain=chain)

        def VRED(out, in_, op, absval=False, waits=(), inc=False):
            return E("vector",
                     lambda eo, o=out, i=in_, p=op, a=absval:
                     eo.tensor_reduce(out=o, in_=i, axis=AX.X, op=p,
                                      apply_absolute_value=a),
                     waits=waits, incs=([(vc, 1)] if inc else ()))

        def VSCAN(out, d0, d1, waits=(), inc=False, chain=True):
            return E("vector",
                     lambda eo, o=out, a=d0, b=d1:
                     eo.tensor_tensor_scan(out=o, data0=a, data1=b, initial=0.0,
                                           op0=ALU.add, op1=ALU.bypass),
                     waits=waits, incs=([(vc, 1)] if inc else ()), chain=chain)

        def VRCP(out, in_, waits=(), inc=False):
            return E("vector", lambda eo, o=out, i=in_: eo.reciprocal(out=o, in_=i),
                     waits=waits, incs=([(vc, 1)] if inc else ()))

        def VNOP(waits=()):
            return E("vector", lambda eo: None, waits=waits, incs=[(vc, 1)])

        def ACT(out, in_, func, bias=None, scale=None, accum=None, waits=(), inc=False,
                chain=True):
            def f(eo, o=out, i=in_, fu=func, b=bias, s=scale, a=accum):
                kw = {}
                if b is not None:
                    kw["bias"] = b
                if s is not None:
                    kw["scale"] = s
                if a is not None:
                    kw["accum_out"] = a
                return eo.activation(out=o, in_=i, func=fu, **kw)
            return E("scalar", f, waits=waits, incs=([(sc, 1)] if inc else ()), chain=chain)

        def SNOP(waits=()):
            return E("scalar", lambda eo: None, waits=waits, incs=[(sc, 1)])

        def join():
            toks = [(tn, tn.v), (vc, vc.v), (sc, sc.v), (gp, gp.v)]
            toks += [(s, s.v) for s in dsem.values()]
            for eng in Prog.ENGINES:
                E(eng, lambda eo: None, waits=toks)
            bank_tok.clear()

        t1 = E("gpsimd", lambda eo: eo.memset(ident[:], 0.0), incs=[(gp, 1)])
        t2 = E("gpsimd", lambda eo: eo.memset(identf[:], 0.0), incs=[(gp, 1)])
        t3 = E("gpsimd", lambda eo: eo.memset(trimask[:], 1.0), incs=[(gp, 1)])
        t4 = E("gpsimd", lambda eo: eo.memset(selm[:], 1.0), incs=[(gp, 1)])
        E("gpsimd", lambda eo: eo.memset(ones_f[:], 1.0))
        E("gpsimd", lambda eo: eo.memset(ones_b[:], 1.0))
        E("gpsimd", lambda eo: eo.memset(ones_col[:], 1.0), incs=[(gp, 1)])
        E("gpsimd", lambda eo: eo.affine_select(out=ident[:], in_=ident[:],
                                                pattern=[[-1, 128]], compare_op=ALU.not_equal,
                                                fill=1.0, base=0, channel_multiplier=1),
          waits=[t1], incs=[(gp, 1)])
        E("gpsimd", lambda eo: eo.affine_select(out=identf[:], in_=identf[:],
                                                pattern=[[-1, 128]], compare_op=ALU.not_equal,
                                                fill=1.0, base=0, channel_multiplier=1),
          waits=[t2], incs=[(gp, 1)])
        E("gpsimd", lambda eo: eo.affine_select(out=trimask[:], in_=trimask[:],
                                                pattern=[[1, 128]], compare_op=ALU.is_ge,
                                                fill=0.0, base=0, channel_multiplier=-1),
          waits=[t3], incs=[(gp, 1)])
        t5 = E("gpsimd", lambda eo: eo.affine_select(out=selm[:], in_=selm[:],
                                                pattern=[[1, M * 128]], compare_op=ALU.is_ge,
                                                fill=0.0, base=0, channel_multiplier=-128),
               waits=[t4], incs=[(gp, 1)])
        tok_cst = E("gpsimd", lambda eo: eo.affine_select(out=selm[:], in_=selm[:],
                                                          pattern=[[-1, M * 128]],
                                                          compare_op=ALU.is_ge,
                                                          fill=0.0, base=127,
                                                          channel_multiplier=128),
                    waits=[t5], incs=[(gp, 1)])

        DMA("sync", wgate_l[:].rearrange("p (c m) -> p c m", c=DC),
            wgate_h.rearrange("(c p) m -> p c m", p=128), sq)
        DMA("sync", normw_row[:], normw_h[None, :], sq)
        DMA("sync", b2row[:], b2_h[None, :], sq)
        DMA("gpsimd", b1row[:], b1_h[None, :], gq)
        for m in range(M):
            DMA("gpsimd", gkw1[:, m * DC * 16:(m + 1) * DC * 16]
                .rearrange("p (c l) -> p c l", c=DC),
                wgk1_h.rearrange("m (c p) l -> m p c l", p=128)[m], gq)
        DMA("gpsimd", gkw1[:, DC * 16 * M:].rearrange("p (c l) -> p c l", c=DC),
            wgk1s_h.rearrange("(c p) l -> p c l", p=128), gq)
        for m in range(M):
            DMA("gpsimd", gkw2[:, m * DK:(m + 1) * DK], wgk2_h[m], gq)
        tok_gkw = DMA("gpsimd", gkw2[:, M * DK:], wgk2s_h[:, :], gq)
        tok_sq0 = (sq, sq.v)

        t_nw = MM(pbf(0)[:, 0:HDV], ones_f[:], normw_row[:], True, True,
                  waits=[tok_sq0, tok_cst] + use_banks(0), inc=True)
        t_b2a = MM(pbf(1)[:, 0:512], ones_f[:], b2row[:, 0:512], True, True,
                   waits=use_banks(1), inc=True)
        t_b2b = MM(pbf(2)[:, 0:512], ones_f[:], b2row[:, 512:1024], True, True,
                   waits=use_banks(2), inc=True)
        for p in range(M):
            for kt in range(H):
                DMA("sync", bgk_sb[:, (p * H + kt):(p * H + kt) + 1],
                    bgk_h[p, kt * 128:(kt + 1) * 128][:, None], sq)
        for kt in range(H):
            DMA("sync", bgk_sb[:, (M * H + kt):(M * H + kt) + 1],
                bgks_h[kt * 128:(kt + 1) * 128][:, None], sq)
        tok_bgk = (sq, sq.v)
        tok_nw = ACT(normw_bc[:], pbf(0)[:, 0:HDV], AF.Copy, waits=[t_nw], inc=True)
        VCPY(b2bc[:, 0:512], pbf(1)[:, 0:512], waits=[t_b2a])
        tok_b2 = VCPY(b2bc[:, 512:1024], pbf(2)[:, 0:512], waits=[t_b2b], inc=True)
        done_banks(tok_nw, 0)
        done_banks(tok_b2, 1, 2)

        xprev = {}
        xprev2 = {}

        def build_xfm(s):
            last = None
            for tt in range(CPS):
                par = tt % 2
                xbdst = x_tmb[:, par * D:(par + 1) * D]
                tok_c = DMA("gpsimd", xbdst,
                            x_h[(s * CPS + tt) * 128:(s * CPS + tt + 1) * 128, :],
                            dsem["s_attnr"], waits=[xprev2.get(par)])
                xprev[par] = tok_c
                for g4 in range(2):
                    tk = None
                    for dc4 in range(4):
                        dc = g4 * 4 + dc4
                        tk = TP(pbb(7)[:, dc4 * 128:(dc4 + 1) * 128],
                                xbdst[:, dc * 128:(dc + 1) * 128],
                                ident[:],
                                waits=[tok_c] + (use_banks(7) if dc4 == 0 else []),
                                inc=(dc4 == 3))
                    dst = xfm[:].rearrange("p (c t) -> p c t", c=DC)[
                        :, g4 * 4:(g4 + 1) * 4, tt * 128:(tt + 1) * 128]
                    last = ACT(dst, pbb(7)[:, 0:512].rearrange("p (a b) -> p a b", a=4),
                               AF.Copy, waits=[tk], inc=True, chain=False)
                    done_banks(last, 7)
                    if g4 == 1:
                        xprev2[par] = tk
            return last

        xprev3 = {}
        xprev4 = {}
        for s in range(NSPAN):
            last = None
            for tt in range(CPS):
                par = tt % 2
                xdst = x_tm[:, par * D:(par + 1) * D]
                tok_x = DMA("sync", xdst,
                            x_h[(s * CPS + tt) * 128:(s * CPS + tt + 1) * 128, :],
                            dsem["s_x"], waits=[xprev3.get(par)])
                for g4 in range(2):
                    tk = None
                    for dc4 in range(4):
                        dc = g4 * 4 + dc4
                        tk = TP(pbf(6)[:, dc4 * 128:(dc4 + 1) * 128],
                                xdst[:, dc * 128:(dc + 1) * 128], identf[:],
                                waits=[tok_x] + (use_banks(6) if dc4 == 0 else []),
                                inc=(dc4 == 3))
                    dst = xfm32[:].rearrange("p (c t) -> p c t", c=DC)[
                        :, g4 * 4:(g4 + 1) * 4, tt * 128:(tt + 1) * 128]
                    last = VCPY(dst, pbf(6)[:, 0:512].rearrange("p (a b) -> p a b", a=4),
                                waits=[tk], inc=True)
                    done_banks(last, 6)
                    if g4 == 1:
                        xprev3[par] = tk
            tok_xfm = last
            tk = None
            for dc in range(DC):
                tk = MM(pbf(0)[0:M, 0:SPAN], wgate_l[:, dc * M:(dc + 1) * M],
                        xfm32[:, dc * SPAN:(dc + 1) * SPAN], dc == 0, dc == DC - 1,
                        waits=[tok_xfm, tok_sq0] + (use_banks(0) if dc == 0 else []),
                        inc=(dc == DC - 1))
            tok_lg = VCPY(lg_fm[:, s * SPAN:(s + 1) * SPAN], pbf(0)[0:M, 0:SPAN],
                          waits=[tk], inc=True)
            done_banks(tok_lg, 0)

        for t16 in range(NTILE):
            tk = TP(pbf(0)[:, 0:M], lg_fm[:, t16 * 128:(t16 + 1) * 128], identf[0:M, 0:M],
                    waits=[(vc, vc.v)] + use_banks(0), inc=True)
            tok_c = VCPY(rt_tm[:, 0:4], pbf(0)[:, 0:M], waits=[tk], inc=True)
            done_banks(tok_c, 0)
            VRED(stat[:, 0:1], rt_tm[:, 0:4], ALU.max)
            VTS(stat[:, 4:8], rt_tm[:, 0:4], stat[:, 0:1], None, ALU.is_equal)
            VSTT(stat[:, 4:8], stat[:, 4:8], -1e30, rt_tm[:, 0:4], ALU.mult, ALU.add)
            VRED(stat[:, 1:2], stat[:, 4:8], ALU.max)
            VTS(stat[:, 4:8], rt_tm[:, 0:4], stat[:, 1:2], None, ALU.is_ge)
            tok_m = VTS(stat[:, 2:3], stat[:, 0:1], -1.0, None, ALU.mult, inc=True)
            tok_e = ACT(stat[:, 8:12], rt_tm[:, 0:4], AF.Exp, bias=stat[:, 2:3],
                        waits=[tok_m], inc=True)
            VTT(stat[:, 12:16], stat[:, 8:12], stat[:, 4:8], ALU.mult, waits=[tok_e])
            VRED(stat[:, 3:4], stat[:, 12:16], ALU.add)
            VRCP(stat[:, 2:3], stat[:, 3:4])
            VCPY(rt_tm[:, 4:8], stat[:, 4:8])
            tok_w = VTS(rt_tm[:, 8:12], stat[:, 12:16], stat[:, 2:3], None, ALU.mult,
                        inc=True)
            TP(pbf(0)[0:4, 0:128], rt_tm[:, 4:8], identf[:], waits=[tok_w] + use_banks(0))
            tk2 = TP(pbf(0)[0:4, 128:256], rt_tm[:, 8:12], identf[:], inc=True)
            VCPY(top2_fm[:, t16 * 128:(t16 + 1) * 128], pbf(0)[0:M, 0:128], waits=[tk2])
            tok_f = VCPY(wpre_fm[:, t16 * 128:(t16 + 1) * 128], pbf(0)[0:M, 128:256],
                         inc=True)
            done_banks(tok_f, 0)

        VSCAN(pos_fm[:], top2_fm[:], top2_fm[:])
        VTS(stat[0:M, 16:17], pos_fm[:, T - 1:T], float(-CAP), None, ALU.add)
        VSTT(kept[:], pos_fm[:], stat[0:M, 16:17], top2_fm[:], ALU.is_gt, ALU.mult)
        tok_route = VTT(weff[:], wpre_fm[:], kept[:], ALU.mult, inc=True)
        if debug:
            DMA("sync", dbg["route"][0], lg_fm[:], dsem["s_attnw"], waits=[tok_route])
            DMA("sync", dbg["route"][1], pos_fm[:], dsem["s_attnw"], waits=[tok_route])
            DMA("gpsimd", dbg["route"][2], kept[:], gq, waits=[tok_route])
            DMA("gpsimd", dbg["route"][3], weff[:], gq, waits=[tok_route])
        join()

        if phases < 2:
            _finish(nc, P, so)
            return nc
        E("vector", lambda eo: eo.memset(S0[:], 0.0), incs=[(vc, 1)])
        state_tok = [{0: (vc, vc.v), 1: (vc, vc.v)} for _ in range(NPATH)]
        oacc_free = {}
        xr_war = {}
        afm_war = {}
        epb1_war = {}
        km_war = {}
        store_war = {}
        sil_war = {}
        gate_tok = {}

        nspan_eff = min(NSPAN, _NSPAN_LIM)

        def prep_span(sp, tok_xfm_sp):
            for m in range(M):
                MM(pbf(m)[:, 0:SPAN], selm[:, m * 128:(m + 1) * 128],
                   weff[:, sp * SPAN:(sp + 1) * SPAN], True, True,
                   waits=[tok_route, tok_cst] + use_banks(m),
                   inc=(m == M - 1))
            tkw = (tn, tn.v)
            for m in range(M):
                t1 = ACT(wBs[:, m * SPAN:(m + 1) * SPAN], pbf(m)[:, 0:SPAN], AF.Copy,
                         waits=[tkw] + ([(vc, vc.v)] if m == 0 else []),
                         inc=True, chain=(m == 0))
                done_banks(t1, m)
            for m in range(M):
                MM(pbf(m)[:, 0:SPAN], selm[:, m * 128:(m + 1) * 128],
                   kept[:, sp * SPAN:(sp + 1) * SPAN], True, True,
                   waits=use_banks(m), inc=(m == M - 1))
            tkr = (tn, tn.v)
            for m in range(M):
                t2 = ACT(rBs[:, m * SPAN:(m + 1) * SPAN], pbf(m)[:, 0:SPAN], AF.Copy,
                         waits=[tkr], inc=True, chain=(m == 0))
                done_banks(t2, m)
            tok_bc_ = (sc, sc.v)
            tok_wq = DMA("gpsimd", wkbuf[:].rearrange("p (c k) -> p c k", c=DC),
                         wq_h.rearrange("(c p) k -> p c k", p=128), dsem["g_wk"],
                         waits=[(tn, tn.v)])
            for kt in range(H):
                qb = 2 + kt % 2
                tk = None
                for dc in range(DC):
                    tk = MM(pbf(qb)[:, 0:SPAN],
                            wkbuf[:, dc * DK + kt * 128: dc * DK + kt * 128 + 128],
                            xfm[:, dc * SPAN:(dc + 1) * SPAN], dc == 0, dc == DC - 1,
                            waits=[tok_wq, tok_xfm_sp] + (use_banks(qb) if dc == 0 else []),
                            inc=(dc == DC - 1))
                tq = ACT(qspan[:, kt * SPAN:(kt + 1) * SPAN], pbf(qb)[:, 0:SPAN], AF.Copy,
                         scale=QSCALE,
                         waits=[tk] + ([(vc, vc.v)] if kt == 0 else []), inc=True)
                done_banks(tq, qb)
            return tok_bc_, (sc, sc.v)

        tok_xfm_next = build_xfm(0)
        for s in range(nspan_eff):
            tok_xfm = tok_xfm_next
            if _A_CUT < 1:
                continue

            if s == 0:
                tok_bc, tok_q = prep_span(0, tok_xfm)
            else:
                tok_bc, tok_q = next_bc, next_q
            if _A_CUT < 3:
                continue

            tok_wg = DMA("gpsimd", wvbuf[:].rearrange("p (c k) -> p c k", c=DC),
                         wg_h.rearrange("(c p) k -> p c k", p=128), dsem["g_wv"],
                         waits=[(tn, tn.v)])
            for tt in range(CPS):
                for half in range(2):
                    bk = 3 + half
                    tk = None
                    for dc in range(DC):
                        tk = MM(pbf(bk)[:, 0:512],
                                xfm[:, dc * SPAN + tt * 128: dc * SPAN + tt * 128 + 128],
                                wvbuf[:, dc * DV + half * 512: dc * DV + half * 512 + 512],
                                dc == 0, dc == DC - 1,
                                waits=[tok_wg, tok_xfm] + (use_banks(bk) if dc == 0 else []),
                                inc=(dc == DC - 1))
                    tv_ = VCPY(ep_b1[:, half * 512:(half + 1) * 512], pbf(bk)[:, 0:512],
                               waits=[tk, epb1_war.get(0), store_war.get(0)], inc=True)
                    ts_ = ACT(ep_b2x[:, half * 512:(half + 1) * 512],
                              pbf(bk)[:, 0:512], AF.Sigmoid,
                              waits=[tk, tv_, afm_war.get(0)], inc=True)
                    done_banks(ts_, bk)
                if _G_CUT >= 3:
                    VTT(silu_sp[:, tt * DV:(tt + 1) * DV], ep_b1[:],
                        ep_b2x[:], ALU.mult,
                        waits=[(sc, sc.v), sil_war.get(0)], inc=True)
            tok_silu = (vc, vc.v)
            if _A_CUT < 4:
                continue

            for p in range(NPATH):
                tk = None
                for tt in range(CPS):
                    for dc in range(DC):
                        tk = MM(pbf(0)[:, tt * 16:(tt + 1) * 16],
                                xfm[:, dc * SPAN + tt * 128: dc * SPAN + tt * 128 + 128],
                                gkw1[:, (p * DC + dc) * 16:(p * DC + dc + 1) * 16],
                                tt == 0 and dc == 0, tt == CPS - 1 and dc == DC - 1,
                                waits=[tok_gkw, tok_xfm] + (use_banks(0) if (tt, dc) == (0, 0) else []),
                                inc=(tt == CPS - 1 and dc == DC - 1))
                tl = VCPY(gklow[:], pbf(0)[:, 0:CPS * 16], waits=[tk], inc=True)
                done_banks(tl, 0)
                tk = None
                for tt in range(CPS):
                    tk = TP(pbb(7)[0:16, tt * 128:(tt + 1) * 128],
                            gklow[:, tt * 16:(tt + 1) * 16], ident[:],
                            waits=[(vc, vc.v)] + (use_banks(7) if tt == 0 else []),
                            inc=(tt == CPS - 1))
                tok_low = VCPY(gklow_fm[:], pbb(7)[0:16, 0:SPAN], waits=[tk], inc=True)
                done_banks(tok_low, 7)
                for kt in range(H):
                    tk = MM(pbf(0)[:, 0:SPAN],
                            gkw2[:, p * DK + kt * 128: p * DK + kt * 128 + 128],
                            gklow_fm[:], True, True, waits=[tok_low] + use_banks(0),
                            inc=True)
                    tsg = ACT(sg5[:, (p * H + kt) * SPAN:(p * H + kt + 1) * SPAN],
                              pbf(0)[:, 0:SPAN], AF.Sigmoid,
                              bias=bgk_sb[:, p * H + kt: p * H + kt + 1],
                              waits=[tk, tok_bgk], inc=True)
                    done_banks(tsg, 0)
            tok_sg = (sc, sc.v)
            if _A_CUT < 5:
                continue

            def srck(p):
                return (wk_h.rearrange("m (c p) k -> m p c k", p=128)[p] if p < M
                        else wks_h.rearrange("(c p) k -> p c k", p=128))

            def srcv(p):
                return (wv_h.rearrange("m (c p) k -> m p c k", p=128)[p] if p < M
                        else wvs_h.rearrange("(c p) k -> p c k", p=128))

            khwar = {}
            tok_wk = DMA("gpsimd", wkbuf[:].rearrange("p (c k) -> p c k", c=DC),
                         srck(0), dsem["g_wk"], waits=[tok_q])
            tok_wv = DMA("gpsimd", wvbuf[:].rearrange("p (c k) -> p c k", c=DC),
                         srcv(0), dsem["g_wv"], waits=[tok_silu])
            for p in range(NPATH):
                tok_ln = ACT(scr_f[:], sg5[:, p * H * SPAN:(p + 1) * H * SPAN], AF.Ln,
                             waits=[tok_sg, (vc, vc.v)], inc=True)
                if p < M:
                    rBp = rBs[:, p * SPAN:(p + 1) * SPAN].unsqueeze(1).broadcast_to(
                        [128, H, SPAN])
                    tok_gh = VSTT(scr_b1[:], scr_f[:], 1.0 / GN, rBp, ALU.mult, ALU.mult,
                                  waits=[tok_ln, tok_bc], inc=True)
                else:
                    tok_gh = VTS(scr_b1[:], scr_f[:], 1.0 / GN, None, ALU.mult,
                                 waits=[tok_ln], inc=True)
                tok_b = None
                for kt in range(H):
                    for ch in range(CPS):
                        cs = slice(kt * SPAN + ch * C, kt * SPAN + (ch + 1) * C)
                        tok_b = VSCAN(scr_f[:, cs], scr_b1[:, cs], scr_b1[:, cs],
                                      waits=[tok_gh],
                                      inc=(kt == H - 1 and ch == CPS - 1),
                                      chain=(kt == 0 and ch == 0))
                ACT(ebc5[:, p * H * CPS:(p + 1) * H * CPS],
                    scr_f[:].rearrange("p (n c) -> p n c", c=C)[:, :, C - 1:C], AF.Exp,
                    waits=[tok_b], inc=True)
                tok_eb = ACT(scr_b1[:], scr_f[:], AF.Exp, waits=[tok_b], inc=True,
                             chain=False)
                tok_emb = ACT(scr_b2[:], scr_f[:], AF.Exp, scale=-1.0, inc=True,
                              chain=False)
                if p < M:
                    wBp = wBs[:, p * SPAN:(p + 1) * SPAN].unsqueeze(1).broadcast_to(
                        [128, H, SPAN])
                    VTT(scr_b1[:], scr_b1[:], wBp, ALU.mult, waits=[tok_eb])
                    VTT(qw[:], scr_b1[:], qspan[:], ALU.mult, waits=[tok_q], inc=True)
                    rBp = rBs[:, p * SPAN:(p + 1) * SPAN].unsqueeze(1).broadcast_to(
                        [128, H, SPAN])
                    tok_embM = VTT(scr_b2[:], scr_b2[:], rBp, ALU.mult,
                                   waits=[tok_emb], inc=True)
                else:
                    VTT(qw[:], scr_b1[:], qspan[:], ALU.mult,
                        waits=[tok_eb, tok_emb, tok_q], inc=True)
                    tok_embM = (vc, vc.v)

                if _A_CUT < 6:
                    continue
                tok_kproj = None
                for kt in range(H):
                    bk = 5 + kt % 2
                    tk = None
                    for dc in range(DC):
                        tk = MM(pbf(bk)[:, 0:SPAN],
                                wkbuf[:, dc * DK + kt * 128: dc * DK + kt * 128 + 128],
                                xfm[:, dc * SPAN:(dc + 1) * SPAN], dc == 0, dc == DC - 1,
                                waits=[tok_wk] + (use_banks(bk) if dc == 0 else []),
                                inc=(dc == DC - 1))
                    tok_kproj = tk
                    te = ACT(ktil[:, kt * SPAN:(kt + 1) * SPAN], pbf(bk)[:, 0:SPAN],
                             AF.Copy,
                             waits=[tk] + ([km_war.get(0)] if kt == 0 else []),
                             inc=True, chain=False)
                    done_banks(te, 5 + kt % 2)
                tok_kmask = VTT(ktil[:], ktil[:], scr_b2[:], ALU.mult,
                                waits=[tok_embM, te], inc=True)
                km_war[0] = tok_kmask
                tok_vproj = None
                for tt in range(CPS):
                    for half in range(2):
                        bk = 5 + half
                        tk = None
                        for dc in range(DC):
                            tk = MM(pbf(bk)[:, 0:512],
                                    xfm[:, dc * SPAN + tt * 128: dc * SPAN + tt * 128 + 128],
                                    wvbuf[:, dc * DV + half * 512: dc * DV + half * 512 + 512],
                                    dc == 0, dc == DC - 1,
                                    waits=[tok_wv] + (use_banks(bk) if dc == 0 else []),
                                    inc=(dc == DC - 1))
                        tok_vproj = tk
                        tv = VCPY(vtil[:, tt * DV + half * 512: tt * DV + half * 512 + 512],
                                  pbf(bk)[:, 0:512], waits=[tk], inc=True,
                                  chain=(tt == 0 and half == 0))
                        done_banks(tv, bk)
                tok_v = (vc, vc.v)
                if p + 1 < NPATH:
                    tok_wk = DMA("gpsimd", wkbuf[:].rearrange("p (c k) -> p c k", c=DC),
                                 srck(p + 1), dsem["g_wk"], waits=[tok_kproj])
                    tok_wv = DMA("gpsimd", wvbuf[:].rearrange("p (c k) -> p c k", c=DC),
                                 srcv(p + 1), dsem["g_wv"], waits=[tok_vproj])

                if _A_CUT < 7:
                    continue
                ktm_all = scr_b1
                asb_all = scr_b2
                for ch in range(CPS):
                    kbuf = khat if ch % 2 == 0 else khat_tm
                    tok_kh = None
                    for hh in range(H):
                        k_ap = ktil[:, hh * SPAN + ch * C: hh * SPAN + ch * C + C]
                        ecol = ebc5[:, (p * H + hh) * CPS + ch:(p * H + hh) * CPS + ch + 1]
                        tok_kh = ACT(kbuf[:, hh * C:(hh + 1) * C], k_ap, AF.Copy,
                                     scale=ecol,
                                     waits=([tok_kmask, khwar.get(ch % 2)] if hh == 0 else []),
                                     inc=(hh == H - 1), chain=(hh == 0 and ch == 0))
                    tk_t = None
                    for hh in range(H):
                        tk_t = TP(pbb(7)[:, hh * C:(hh + 1) * C],
                                  kbuf[:, hh * C:(hh + 1) * C], ident[:],
                                  waits=[tok_kh] + (use_banks(7) if hh == 0 else []),
                                  inc=(hh == H - 1))
                    khwar[ch % 2] = tk_t
                    tktm = VCPY(ktm_all[:, ch * 512:(ch + 1) * 512], pbb(7)[:, 0:512],
                                waits=[tk_t], inc=True, chain=(ch == 0))
                    done_banks(tktm, 7)
                for ch in range(CPS):
                    tk_a = None
                    for hh in range(H):
                        q_ap = qw[:, hh * SPAN + ch * C: hh * SPAN + ch * C + C]
                        k_ap = ktil[:, hh * SPAN + ch * C: hh * SPAN + ch * C + C]
                        tk_a = MM(pbf(2)[:, hh * C:(hh + 1) * C], k_ap, q_ap,
                                  hh == 0, hh == H - 1,
                                  waits=([tok_kmask] if hh == 0 else [])
                                        + (use_banks(2) if hh == 0 else []),
                                  inc=(hh == H - 1))
                    trim4 = trimask[:, :].unsqueeze(1).broadcast_to([128, H, C])
                    tok_am = VTT(asb_all[:, ch * 512:(ch + 1) * 512], pbf(2)[:, 0:512],
                                 trim4, ALU.mult, waits=[tk_a], inc=True, chain=False)
                    done_banks(tok_am, 2)
                tok_pre = (vc, vc.v)
                for ch in range(CPS):
                    gchunk = s * CPS + ch
                    Sprev = S0 if gchunk % 2 == 0 else S1
                    Snext = S1 if gchunk % 2 == 0 else S0
                    tk_s = None
                    for hh in range(H):
                        v_ap = vtil[:, ch * DV + hh * HDV: ch * DV + hh * HDV + HDV]
                        bk = 3 + hh // 2
                        osl = slice((hh % 2) * HDV, (hh % 2) * HDV + HDV)
                        tk_s = MM(pbf(bk)[:, osl],
                                  ktm_all[:, ch * 512 + hh * C: ch * 512 + (hh + 1) * C],
                                  v_ap, hh % 2 == 0, hh % 2 == 1,
                                  waits=[tok_pre] + (use_banks(3, 4) if hh == 0 else []),
                                  inc=(hh == H - 1))
                    for hh in range(H):
                        bk = 3 + hh // 2
                        osl = slice((hh % 2) * HDV, (hh % 2) * HDV + HDV)
                        ecol = ebc5[:, (p * H + hh) * CPS + ch:(p * H + hh) * CPS + ch + 1]
                        sp_ap = Sprev[:, (p * H + hh) * HDV:(p * H + hh + 1) * HDV]
                        sn_ap = Snext[:, (p * H + hh) * HDV:(p * H + hh + 1) * HDV]
                        tms = VSTT(sn_ap, sp_ap, ecol, pbf(bk)[:, osl], ALU.mult, ALU.add,
                                   waits=[tk_s], inc=(hh == H - 1), chain=(hh == 0))
                    done_banks(tms, 3, 4)
                    state_tok[p][(ch + 1) % 2] = tms
                    for hh in range(H):
                        q_ap = qw[:, hh * SPAN + ch * C: hh * SPAN + ch * C + C]
                        s_ap = Sprev[:, (p * H + hh) * HDV:(p * H + hh + 1) * HDV]
                        bk = hh // 2
                        osl = slice((hh % 2) * HDV, (hh % 2) * HDV + HDV)
                        MM(pbf(bk)[:, osl], q_ap, s_ap, hh % 2 == 0, False,
                           waits=[tok_v, state_tok[p][gchunk % 2]]
                                 + (use_banks(0, 1) if hh == 0 else []))
                    tk_o = None
                    for hh in range(H):
                        v_ap = vtil[:, ch * DV + hh * HDV: ch * DV + hh * HDV + HDV]
                        bk = hh // 2
                        osl = slice((hh % 2) * HDV, (hh % 2) * HDV + HDV)
                        tk_o = MM(pbf(bk)[:, osl],
                                  asb_all[:, ch * 512 + hh * C: ch * 512 + (hh + 1) * C],
                                  v_ap, False, hh % 2 == 1,
                                  waits=[tok_pre], inc=(hh == H - 1))
                    for half in range(2):
                        osl = slice(ch * DV + half * 512, ch * DV + half * 512 + 512)
                        if p == 0:
                            to = VCPY(o_acc[:, osl], pbf(half)[:, 0:512],
                                      waits=[tk_o] + (oacc_free.get(ch) or []),
                                      inc=True, chain=(half == 0))
                        else:
                            to = VTT(o_acc[:, osl], o_acc[:, osl], pbf(half)[:, 0:512],
                                     ALU.add, waits=[tk_o], inc=True, chain=(half == 0))
                        done_banks(to, half)

            if _A_CUT < 8:
                join()
                continue
            tok_wo = DMA("gpsimd", wvbuf[:].rearrange("p (c k) -> p c k", c=DC),
                         wo_h.rearrange("(c p) k -> p c k", p=128), dsem["g_wo"],
                         waits=[tok_vproj])
            tok_xfm_next = build_xfm(s + 1) if s + 1 < nspan_eff else None
            if s + 1 < nspan_eff:
                next_bc, next_q = prep_span(s + 1, tok_xfm_next)
            for ch in range(CPS):
                tok_oacc = (vc, vc.v)
                tok_st = None
                for hh in range(H):
                    osl = slice(ch * DV + hh * HDV, ch * DV + hh * HDV + HDV)
                    tok_st = ACT(ep_f1[:, hh * HDV:(hh + 1) * HDV], o_acc[:, osl], AF.Square,
                                 accum=stat[:, hh:hh + 1], waits=[tok_oacc],
                                 inc=(hh == H - 1), chain=(hh == 0))
                tv1 = VTS(stat[:, 4:8], stat[:, 0:4], 1.0 / HDV, EPS, ALU.mult, ALU.add,
                          waits=[tok_st], inc=True)
                ACT(stat[:, 4:8], stat[:, 4:8], AF.Ln, waits=[tv1])
                tok_rs = ACT(stat[:, 8:12], stat[:, 4:8], AF.Exp, scale=-0.5, inc=True)
                nwrep = normw_bc[:, :].unsqueeze(1).broadcast_to([128, H, HDV])
                VTT(ep_f1[:], o_acc[:, ch * DV:(ch + 1) * DV], nwrep, ALU.mult,
                    waits=[tok_nw, tok_rs])
                tok_a = None
                for hh in range(H):
                    ssl = slice(ch * DV + hh * HDV, ch * DV + hh * HDV + HDV)
                    tok_a = VSTT(silu_sp[:, ssl],
                                 ep_f1[:, hh * HDV:(hh + 1) * HDV],
                                 stat[:, 8 + hh:9 + hh], silu_sp[:, ssl], ALU.mult, ALU.mult,
                                 waits=[tok_silu],
                                 inc=(hh == H - 1), chain=(hh == 0))
                gate_tok[ch] = tok_a
            s1_tok = {}
            for ch in range(CPS):
                gchunk = s * CPS + ch
                xr = scr_f[:, (ch % 2) * D:(ch % 2) * D + D]
                tok_xr = DMA("sync", xr, x_h[gchunk * 128:(gchunk + 1) * 128, :],
                             dsem["s_xr"], waits=[xr_war.get(ch % 2), tok_emb])
                for g4 in range(2):
                    tk = None
                    for dc4 in range(4):
                        dc = g4 * 4 + dc4
                        tk = TP(pbb(7)[:, dc4 * 128:(dc4 + 1) * 128],
                                silu_sp[:, ch * DV + dc * 128:ch * DV + (dc + 1) * 128],
                                ident[:],
                                waits=[gate_tok[ch]] + (use_banks(7) if dc4 == 0 else []),
                                inc=(dc4 == 3))
                    ta_ = ACT(afm[:, g4 * 512:(g4 + 1) * 512], pbb(7)[:, 0:512],
                              AF.Copy, waits=[tk, afm_war.get(0)], inc=True,
                              chain=False)
                    done_banks(ta_, 7)
                    if g4 == 1 and ch == CPS - 1:
                        sil_war[0] = tk
                tok_afm = (sc, sc.v)
                ob0 = 0 if ch % 2 == 0 else 5
                tk_y = None
                for half in range(2):
                    for dc in range(DC):
                        tk_y = MM(pbf(ob0 + half)[:, 0:512], afm[:, dc * 128:(dc + 1) * 128],
                                  wvbuf[:, dc * D + half * 512: dc * D + half * 512 + 512],
                                  dc == 0, dc == DC - 1,
                                  waits=[tok_afm, tok_wo]
                                        + (use_banks(ob0 + half) if dc == 0 else []),
                                  inc=(half == 1 and dc == DC - 1))
                afm_war[0] = tk_y
                x1b = o_acc[:, ch * DV:(ch + 1) * DV]
                tok_s1 = None
                for half in range(2):
                    tok_s1 = VTT(x1b[:, half * 512:(half + 1) * 512],
                                 xr[:, half * 512:(half + 1) * 512],
                                 pbf(ob0 + half)[:, 0:512],
                                 ALU.add, waits=[tk_y, tok_xr], inc=(half == 1))
                    done_banks(tok_s1, ob0 + half)
                xr_war[ch % 2] = tok_s1
                s1_tok[ch] = tok_s1
            for ch in range(CPS):
                gchunk = s * CPS + ch
                x1b = o_acc[:, ch * DV:(ch + 1) * DV]
                ACT(ep_f1[:], x1b, AF.Identity, accum=stat[:, 12:13], waits=[s1_tok[ch]])
                tok_ms = ACT(ep_f1[:], x1b, AF.Square, accum=stat[:, 13:14], inc=True)
                VTS(stat[:, 14:15], stat[:, 12:13], 1.0 / D, None, ALU.mult, waits=[tok_ms])
                VTT(stat[:, 15:16], stat[:, 14:15], stat[:, 14:15], ALU.mult)
                VSTT(stat[:, 16:17], stat[:, 13:14], 1.0 / D, stat[:, 15:16],
                     ALU.mult, ALU.subtract)
                tv2 = VTS(stat[:, 17:18], stat[:, 16:17], EPS, None, ALU.add, inc=True)
                ACT(stat[:, 17:18], stat[:, 17:18], AF.Ln, waits=[tv2])
                tok_r1 = ACT(stat[:, 18:19], stat[:, 17:18], AF.Exp, scale=-0.5, inc=True)
                VTS(stat[:, 19:20], stat[:, 14:15], -1.0, None, ALU.mult, waits=[tok_r1])
                VTS(x1b, x1b, stat[:, 19:20], None, ALU.add)
                tok_x1 = VTS(x1b, x1b, stat[:, 18:19], None, ALU.mult, inc=True)
                tok_x1d = DMA("gpsimd", x1_hbm[gchunk * 128:(gchunk + 1) * 128, :], x1b,
                              dsem["s_x1w"], waits=[tok_x1])
                VRED(stat[:, 20:21], x1b, ALU.max, absval=True, waits=[tok_x1])
                VTS(stat[:, 21:22], stat[:, 16:17], EPS, None, ALU.add)
                VRCP(stat[:, 22:23], stat[:, 21:22])
                VTT(stat[:, 23:24], stat[:, 16:17], stat[:, 22:23], ALU.mult)
                tv3 = VTS(stat[:, 23:24], stat[:, 23:24], EPS, None, ALU.add, inc=True)
                ACT(stat[:, 23:24], stat[:, 23:24], AF.Ln, waits=[tv3])
                tok_rq = ACT(stat[:, 24:25], stat[:, 23:24], AF.Exp, scale=-0.5, inc=True)
                VTT(stat[:, 25:26], stat[:, 20:21], stat[:, 24:25], ALU.mult, waits=[tok_rq])
                VTS(stat[:, 25:26], stat[:, 25:26], 1e-5, None, ALU.max)
                VTS(stat[:, 27:28], stat[:, 25:26], 1.0 / 127.0, None, ALU.mult)
                VRCP(stat[:, 26:27], stat[:, 25:26])
                VTT(stat[:, 26:27], stat[:, 26:27], stat[:, 24:25], ALU.mult)
                VTS(stat[:, 26:27], stat[:, 26:27], 127.0, None, ALU.mult)
                if not _SIM_BUILD:
                    VTS(ep_i8[:], x1b, stat[:, 26:27], None, ALU.mult)
                    VCPY(ep_b1[:], ep_i8[:],
                         waits=[epb1_war.get(0), store_war.get(0)])
                else:
                    for qh in range(2):
                        qs = slice(qh * 512, (qh + 1) * 512)
                        zb = ep_f1[:, 0:512]
                        rb = ep_f1[:, 512:1024]
                        VTS(zb, x1b[:, qs], stat[:, 26:27], None, ALU.mult)
                        VTS(rb, zb, 1.0, None, ALU.mod)
                        E("vector", lambda eo: eo.scalar_tensor_tensor(
                            out=rb, in0=rb, scalar=0.5, in1=rb,
                            op0=ALU.is_ge, op1=ALU.subtract))
                        VTT(ep_b1[:, qs], zb, rb, ALU.add,
                            waits=([epb1_war.get(0), store_war.get(0)]
                                   if qh == 0 else ()))
                tok_q8 = VTS(ep_b1[:], ep_b1[:], stat[:, 27:28], None, ALU.mult, inc=True)
                oacc_free[ch] = [tok_q8, tok_x1d]
                for g4 in range(2):
                    tk = None
                    for dc4 in range(4):
                        dc = g4 * 4 + dc4
                        tk = TP(pbb(7)[:, dc4 * 128:(dc4 + 1) * 128],
                                ep_b1[:, dc * 128:(dc + 1) * 128], ident[:],
                                waits=[tok_q8] + (use_banks(7) if dc4 == 0 else []),
                                inc=(dc4 == 3))
                    tq8 = VCPY(ep_b1[:, g4 * 512:(g4 + 1) * 512], pbb(7)[:, 0:512],
                               waits=[tk], inc=True)
                    done_banks(tq8, 7)
                    if g4 == 1:
                        epb1_war[0] = tk
                tok_x1q = DMA(
                    "gpsimd",
                    x1q_hbm.rearrange("(c p) t -> p c t", p=128)[
                        :, :, gchunk * 128:(gchunk + 1) * 128],
                    ep_b1[:].rearrange("p (c t) -> p c t", c=DC), dsem["s_x1qw"],
                    waits=[(vc, vc.v)])
                store_war[0] = tok_x1q
        join()

        if phases < 3:
            _finish(nc, P, so)
            return nc

        if phases < 4:
            _finish(nc, P, so)
            return nc
        for wi in range(1):
            rtoks = {}
            for cch in range(32):
                par = cch % 4
                dst = fwbuf[:, par * D:(par + 1) * D]
                src = w1_h[cch * 128:(cch + 1) * 128, :]
                qn = ("s_fw", "s_x", "s_xr")[cch % 3]
                tok_l = DMA("sync", dst, src, dsem[qn], waits=[rtoks.get(par)])
                rtoks[par] = VRED(wred[:, cch:cch + 1], dst, ALU.add, absval=True,
                                  waits=[tok_l], inc=True)
            tok_t = VRED(wred[:, 32:33], wred[:, 0:32], ALU.add, waits=[(vc, vc.v)],
                         inc=True)
            tk = MM(pbf(2)[0:1, 0:1], wred[:, 32:33], ones_col[:], True, True,
                    waits=[tok_t] + use_banks(2), inc=True)
            tok_m1 = VTS(stat[0:1, 0:1], pbf(2)[0:1, 0:1], 1.0 / float(FF * D), None,
                         ALU.mult, waits=[tk], inc=True)
            done_banks(tok_m1, 2)
            VTS(stat[0:1, 1:2], stat[0:1, 0:1], 1e-5, None, ALU.max)
            tok_m2 = VRCP(stat[0:1, 2:3], stat[0:1, 1:2], inc=True)
            MM(pbf(2)[:, 0:1], ones_f[:], stat[0:1, 2:3], True, True,
               waits=[tok_m2] + use_banks(2))
            tk = MM(pbf(2)[:, 1:2], ones_f[:], stat[0:1, 1:2], True, True, inc=True)
            tbc = VCPY(scalew[:, 2 * wi:2 * wi + 2], pbf(2)[:, 0:2], waits=[tk], inc=True)
            done_banks(tbc, 2)

        qtoks = {}
        for cch in range(32):
            par = cch % 4
            fb = fwbuf[:, par * D:(par + 1) * D]
            qn = ("s_fw", "s_x", "s_xr")[cch % 3]
            tok_l = DMA("sync", fb, w1_h[cch * 128:(cch + 1) * 128, :], dsem[qn],
                        waits=[qtoks.get(par), rtoks.get(par)])
            if not _SIM_BUILD:
                VTS(fqi8[:], fb, scalew[:, 0:1], None, ALU.mult, waits=[tok_l, (vc, vc.v)])
                VTS(fqbf[:], fqi8[:], 1.0, -1.0, ALU.min, ALU.max)
            else:
                VTS(simz[:], fb, scalew[:, 0:1], None, ALU.mult, waits=[tok_l, (vc, vc.v)])
                VTS(simr[:], simz[:], 1.0, None, ALU.mod)
                E("vector", lambda eo: eo.scalar_tensor_tensor(
                    out=simr[:], in0=simr[:], scalar=0.5, in1=simr[:],
                    op0=ALU.is_ge, op1=ALU.subtract))
                VTT(simz[:], simz[:], simr[:], ALU.add)
                VTS(fqbf[:], simz[:], 1.0, -1.0, ALU.min, ALU.max)
            tok_qn = VTS(fqbf[:], fqbf[:], scalew[:, 1:2], None, ALU.mult, inc=True)
            for g4 in range(2):
                tk = None
                for dc4 in range(4):
                    dc = g4 * 4 + dc4
                    tk = TP(pbb(7)[:, dc4 * 128:(dc4 + 1) * 128],
                            fqbf[:, dc * 128:(dc + 1) * 128], ident[:],
                            waits=[tok_qn] + (use_banks(7) if dc4 == 0 else []),
                            inc=(dc4 == 3))
                dst = w1T[:].rearrange("p (c f) -> p c f", c=DC)[
                    :, g4 * 4:(g4 + 1) * 4, cch * 128:(cch + 1) * 128]
                tw = VCPY(dst, pbb(7)[:, 0:512].rearrange("p (a b) -> p a b", a=4),
                          waits=[tk], inc=True)
                done_banks(tw, 7)
                if g4 == 1:
                    qtoks[par] = tk
        tok_wT = (vc, vc.v)
        join()

        if phases < 5:
            _finish(nc, P, so)
            return nc
        ftoks = {}
        f1_hq_war = {}
        f1_st_war = {}
        w2rtoks = {}
        for tt in range(NTILE):
            par = tt % 2
            xq = x1q_t[:, par * D:(par + 1) * D]
            tok_xq = DMA("sync", xq.rearrange("p (c t) -> p c t", c=DC),
                         x1q_hbm.rearrange("(c p) t -> p c t", p=128)[
                             :, :, tt * 128:(tt + 1) * 128], dsem["s_x1qr"],
                         waits=[ftoks.get(par)])
            for fc in range(8):
                bk = fc % 2
                for dc in range(DC):
                    MM(pbf(bk)[:, 0:512], xq[:, dc * 128:(dc + 1) * 128],
                       w1T[:, dc * FF + fc * 512: dc * FF + fc * 512 + 512],
                       dc == 0, False,
                       waits=[tok_xq, tok_wT] + (use_banks(bk) if dc == 0 else []))
                tk = MM(pbf(bk)[:, 0:512], ones_b[:], b1row[:, fc * 512:(fc + 1) * 512],
                        False, True, inc=True)
                if fc == 7:
                    ftoks[par] = tk
                th = ACT(h_bf[:, fc * 512:(fc + 1) * 512], pbf(bk)[:, 0:512], AF.Gelu,
                         waits=[tk] + ([f1_hq_war.get(0)] if fc == 0 else []),
                         inc=True, chain=(fc == 0))
                done_banks(th, bk)
            tok_h = (sc, sc.v)
            if debug and tt == 0:
                DMA("sync", dbg["h0"][:, :], h_bf[:], dsem["s_attnw"], waits=[tok_h])
                DMA("sync", dbg["w1T"][:, :], w1T[:], dsem["s_attnw"], waits=[tok_h])
            tok_am = VRED(stat[:, 0:1], h_bf[:], ALU.max, absval=True,
                          waits=[tok_h, (sc, sc.v)], inc=True)
            tok_sq2 = ACT(hsq[:], h_bf[:], AF.Square, accum=stat[:, 1:2],
                          waits=[tok_h, f1_st_war.get(0)], inc=True)
            VCPY(fstash[:, 2 * tt:2 * tt + 1], stat[:, 0:1], waits=[tok_am, tok_sq2])
            VCPY(fstash[:, 2 * tt + 1:2 * tt + 2], stat[:, 1:2])
            VTS(stat[:, 2:3], stat[:, 0:1], 1e-20, None, ALU.max)
            VRCP(stat[:, 3:4], stat[:, 2:3])
            tok_cq = VTS(stat[:, 3:4], stat[:, 3:4], 127.0, None, ALU.mult, inc=True)
            if not _SIM_BUILD:
                VTS(hq_i8[:], h_bf[:], stat[:, 3:4], None, ALU.mult, waits=[tok_cq])
                tok_hq = VCPY(hsq[:], hq_i8[:], inc=True)
            else:
                for qc in range(4):
                    qs = slice(qc * D, (qc + 1) * D)
                    VTS(simz[:], h_bf[:, qs], stat[:, 3:4], None, ALU.mult,
                        waits=([tok_cq] if qc == 0 else ()))
                    VTS(simr[:], simz[:], 1.0, None, ALU.mod)
                    E("vector", lambda eo: eo.scalar_tensor_tensor(
                        out=simr[:], in0=simr[:], scalar=0.5, in1=simr[:],
                        op0=ALU.is_ge, op1=ALU.subtract))
                    VTT(hsq[:, qs], simz[:], simr[:], ALU.add)
                tok_hq = VNOP()
            f1_hq_war[0] = tok_hq
            f1_st_war[0] = DMA("gpsimd", hq_hbm[tt * 128:(tt + 1) * 128, :], hsq[:],
                               dsem["s_m2w"], waits=[tok_hq])
            for c2 in (2 * tt, 2 * tt + 1):
                par2 = c2 % 2
                dj2, aa2 = c2 // 4, c2 % 4
                dst2 = f2wbuf[:, par2 * D:(par2 + 1) * D]
                tok_l2 = DMA("sync", dst2,
                             w2_h[dj2 * 128:(dj2 + 1) * 128, aa2 * D:(aa2 + 1) * D],
                             dsem["s_fw"], waits=[w2rtoks.get(par2)])
                w2rtoks[par2] = VRED(wred[:, c2:c2 + 1], dst2, ALU.add, absval=True,
                                     waits=[tok_l2], inc=True)
        if debug:
            DMA("sync", dbg["fstash"][:, :], fstash[:], dsem["s_attnw"], waits=[(vc, vc.v)])
        join()

        rtoks = w2rtoks
        tok_t = VRED(wred[:, 32:33], wred[:, 0:32], ALU.add, waits=[(vc, vc.v)],
                     inc=True)
        tk = MM(pbf(2)[0:1, 0:1], wred[:, 32:33], ones_col[:], True, True,
                waits=[tok_t] + use_banks(2), inc=True)
        tok_m1 = VTS(stat[0:1, 0:1], pbf(2)[0:1, 0:1], 1.0 / float(FF * D), None,
                     ALU.mult, waits=[tk], inc=True)
        done_banks(tok_m1, 2)
        VTS(stat[0:1, 1:2], stat[0:1, 0:1], 1e-5, None, ALU.max)
        tok_m2s = VRCP(stat[0:1, 2:3], stat[0:1, 1:2], inc=True)
        MM(pbf(2)[:, 0:1], ones_f[:], stat[0:1, 2:3], True, True,
           waits=[tok_m2s] + use_banks(2))
        tk = MM(pbf(2)[:, 1:2], ones_f[:], stat[0:1, 1:2], True, True, inc=True)
        tbc = VCPY(scalew[:, 2:4], pbf(2)[:, 0:2], waits=[tk], inc=True)
        done_banks(tbc, 2)
        qtoks = {}
        for cch in range(32):
            par = cch % 2
            dj, aa = cch // 4, cch % 4
            fb = f2wbuf[:, par * D:(par + 1) * D]
            tok_l = DMA("sync", fb, w2_h[dj * 128:(dj + 1) * 128, aa * D:(aa + 1) * D],
                        dsem["s_fw"], waits=[qtoks.get(par)])
            if not _SIM_BUILD:
                VTS(f2qi8[:], fb, scalew[:, 2:3], None, ALU.mult,
                    waits=[tok_l, (vc, vc.v)])
                tok_qn = VTS(f2qbf[:], f2qi8[:], 1.0, -1.0, ALU.min, ALU.max, inc=True)
            else:
                VTS(simz2[:], fb, scalew[:, 2:3], None, ALU.mult,
                    waits=[tok_l, (vc, vc.v)])
                VTS(simr2[:], simz2[:], 1.0, None, ALU.mod)
                E("vector", lambda eo: eo.scalar_tensor_tensor(
                    out=simr2[:], in0=simr2[:], scalar=0.5, in1=simr2[:],
                    op0=ALU.is_ge, op1=ALU.subtract))
                VTT(simz2[:], simz2[:], simr2[:], ALU.add)
                tok_qn = VTS(f2qbf[:], simz2[:], 1.0, -1.0, ALU.min, ALU.max, inc=True)
            for g4 in range(2):
                tk = None
                for fs4 in range(4):
                    fs = g4 * 4 + fs4
                    tk = TP(pbb(7)[:, fs4 * 128:(fs4 + 1) * 128],
                            f2qbf[:, fs * 128:(fs + 1) * 128], ident[:],
                            waits=[tok_qn] + (use_banks(7) if fs4 == 0 else []),
                            inc=(fs4 == 3))
                dst = w2T[:].rearrange("p (c f) -> p c f", c=FF // 128)[
                    :, aa * 8 + g4 * 4: aa * 8 + (g4 + 1) * 4, dj * 128:(dj + 1) * 128]
                tw = VCPY(dst, pbb(7)[:, 0:512].rearrange("p (a b) -> p a b", a=4),
                          waits=[tk], inc=True)
                done_banks(tw, 7)
                if g4 == 1:
                    qtoks[par] = tk
        tok_wT2 = (vc, vc.v)
        fintoks = {}
        hq2war = {}
        f2_st_war = {}
        if debug:
            DMA("sync", dbg["scalew"][:, :], scalew[:], dsem["s_attnw"], waits=[(vc, vc.v)])
        join()
        def emit_final(ptt, ppar, ptm2):
            m2v = m2_t[:, ppar * D:(ppar + 1) * D]
            x1v = fin_x1b[:, ppar * D:(ppar + 1) * D]
            tok_x = DMA("sync", x1v, x1_hbm[ptt * 128:(ptt + 1) * 128, :], dsem["s_x1r"],
                        waits=[fintoks.get(ppar)])
            tv = VTS(stat[:, 0:1], fstash[:, 2 * ptt + 1:2 * ptt + 2], 1.0 / FF, EPS,
                     ALU.mult, ALU.add, inc=True)
            ACT(stat[:, 0:1], stat[:, 0:1], AF.Ln, waits=[tv])
            tok_rq = ACT(stat[:, 1:2], stat[:, 0:1], AF.Exp, scale=-0.5, inc=True)
            VTT(stat[:, 2:3], fstash[:, 2 * ptt:2 * ptt + 1], stat[:, 1:2], ALU.mult,
                waits=[tok_rq])
            VTS(stat[:, 2:3], stat[:, 2:3], 1e-5, None, ALU.max)
            VTT(stat[:, 2:3], stat[:, 2:3], scalew[:, 3:4], ALU.mult)
            VTS(stat[:, 2:3], stat[:, 2:3], 1.0 / 127.0, None, ALU.mult)
            VTS(m2v, m2v, stat[:, 2:3], None, ALU.mult, waits=[ptm2])
            VTT(m2v, m2v, b2bc[:], ALU.add)
            tok_s2 = VTT(m2v, m2v, x1v, ALU.add, waits=[tok_x], inc=True)
            ACT(fin_ob[:], m2v, AF.Identity, accum=stat[:, 4:5],
                waits=[tok_s2, f2_st_war.get(0)])
            tok_ms = ACT(fin_ob[:], m2v, AF.Square, accum=stat[:, 5:6], inc=True)
            VTS(stat[:, 6:7], stat[:, 4:5], 1.0 / D, None, ALU.mult, waits=[tok_ms])
            VTT(stat[:, 7:8], stat[:, 6:7], stat[:, 6:7], ALU.mult)
            VSTT(stat[:, 8:9], stat[:, 5:6], 1.0 / D, stat[:, 7:8], ALU.mult, ALU.subtract)
            tv2 = VTS(stat[:, 9:10], stat[:, 8:9], EPS, None, ALU.add, inc=True)
            ACT(stat[:, 9:10], stat[:, 9:10], AF.Ln, waits=[tv2])
            tok_r2 = ACT(stat[:, 10:11], stat[:, 9:10], AF.Exp, scale=-0.5, inc=True)
            VTS(stat[:, 11:12], stat[:, 6:7], -1.0, None, ALU.mult, waits=[tok_r2])
            VTS(fin_ob[:], m2v, stat[:, 11:12], None, ALU.add)
            tok_o = VTS(fin_ob[:], fin_ob[:], stat[:, 10:11], None, ALU.mult, inc=True)
            f2_st_war[0] = DMA("gpsimd", out_h[ptt * 128:(ptt + 1) * 128, :], fin_ob[:],
                               so, waits=[tok_o])
            fintoks[ppar] = tok_s2

        pend = None
        for tt in range(NTILE + 1):
            if tt < NTILE:
                par = tt % 2
                hqb = hq2_bf[:, par * FF:(par + 1) * FF]
                hfm = hq_fm[:, par * FF:(par + 1) * FF]
                tok_hb = DMA("sync", hqb, hq_hbm[tt * 128:(tt + 1) * 128, :],
                             dsem["s_x1qr"], waits=[hq2war.get(par)])
                for g8 in range(FF // 128 // 4):
                    tb = g8 % 2
                    tk = None
                    for f4 in range(4):
                        fi = g8 * 4 + f4
                        tk = MM(pbf(tb)[:, f4 * 128:(f4 + 1) * 128],
                                hqb[:, fi * 128:(fi + 1) * 128], ident[:],
                                True, True,
                                waits=[tok_hb] + (use_banks(tb) if f4 == 0 else []),
                                inc=(f4 == 3))
                    thf = ACT(hfm[:, g8 * 512:(g8 + 1) * 512], pbf(tb)[:, 0:512],
                              AF.Copy, waits=[tk], inc=True, chain=False)
                    done_banks(thf, tb)
                tok_hqf = (sc, sc.v)
                hq2war[par] = tok_hqf
                tk_y2 = None
                b0 = 2 + 2 * par
                for half in range(2):
                    bk = b0 + half
                    for fi in range(FF // 128):
                        tk_y2 = MM(pbf(bk)[:, 0:512], hfm[:, fi * 128:(fi + 1) * 128],
                                   w2T[:, fi * D + half * 512: fi * D + half * 512 + 512],
                                   fi == 0, fi == FF // 128 - 1,
                                   waits=[tok_hqf, tok_wT2] + (use_banks(bk) if fi == 0 else []),
                                   inc=(half == 1 and fi == FF // 128 - 1))
                m2v_ = m2_t[:, par * D:(par + 1) * D]
                ACT(m2v_[:, 0:512], pbf(b0)[:, 0:512], AF.Copy, waits=[tk_y2],
                    chain=False)
                tm2 = ACT(m2v_[:, 512:1024], pbf(b0 + 1)[:, 0:512], AF.Copy, inc=True)
                done_banks(tm2, b0, b0 + 1)
                this = (tt, par, tm2)
            else:
                this = None
            if pend is not None:
                emit_final(*pend)
            pend = this

        _finish(nc, P, so)
    return nc


def _finish(nc, P, so):
    for eng in Prog.ENGINES:
        if eng == "sync":
            continue
        P.emit(eng, lambda eo: None,
               waits=[(s, s.v) for s in []])
    P.emit("sync", lambda eo: None, waits=[(so, so.v)])
    with nc.Block() as block:
        @block.tensor
        def _(eo):
            for f in P.streams["tensor"]:
                f(eo)

        @block.vector
        def _(eo):
            for f in P.streams["vector"]:
                f(eo)

        @block.scalar
        def _(eo):
            for f in P.streams["scalar"]:
                f(eo)

        @block.gpsimd
        def _(eo):
            for f in P.streams["gpsimd"]:
                f(eo)

        @block.sync
        def _(eo):
            for f in P.streams["sync"]:
                f(eo)


_IN_ORDER = [
    ("x", "L0_attn_x"), ("w_gate", "L0_attn_w_gate"), ("w_q", "L0_attn_w_q"),
    ("w_k", "L0_attn_w_k"), ("w_v", "L0_attn_w_v"), ("w_gk1", "L0_attn_w_gk1"),
    ("w_gk2", "L0_attn_w_gk2"), ("b_gk", "L0_attn_b_gk"), ("w_ks", "L0_attn_w_ks"),
    ("w_vs", "L0_attn_w_vs"), ("w_gk1s", "L0_attn_w_gk1s"), ("w_gk2s", "L0_attn_w_gk2s"),
    ("b_gks", "L0_attn_b_gks"), ("w_g", "L0_attn_w_g"), ("w_o", "L0_attn_w_o"),
    ("norm_w", "L0_attn_norm_w"), ("ffn_w1", "L0_ffn_w1"), ("ffn_b1", "L0_ffn_b1"),
    ("ffn_w2", "L0_ffn_w2"), ("ffn_b2", "L0_ffn_b2"),
]


def kernel(**inputs):
    nc = build_nc(debug=False)
    base = {}
    for dev_name, in_name in _IN_ORDER:
        base[dev_name] = np.ascontiguousarray(np.asarray(inputs[in_name]))
    x = base.pop("x")
    in_maps = []
    for i in range(8):
        d = dict(base)
        d["x"] = np.ascontiguousarray(x[i])
        in_maps.append(d)
    res = run_bass_kernel_spmd(nc, in_maps, core_ids=list(range(8)))
    out = np.stack([res.results[i]["out"] for i in range(8)], axis=0)
    return out.astype(np.float32)
